# version 04
# baseline (speedup 1.0000x reference)
import os
import numpy as np

import concourse.bass as bass
import concourse.mybir as mybir
from concourse.bass_utils import run_bass_kernel_spmd

F32 = mybir.dt.float32
BF16 = mybir.dt.bfloat16
U8 = mybir.dt.uint8
AF = mybir.ActivationFunctionType
ALU = mybir.AluOpType
AX = mybir.AxisListType

S = 4096
DM = 2048
FQ = 6144
NH = 16
NE = 16
HF = 1408
HSH = 2816
NDC = DM // 128
NFC = FQ // 128
NHC = HF // 128
NSC = HSH // 128
NOT = S // 512
QSCALE = float(1.0 / np.sqrt(np.float32(128.0)))
EXPSH = -10.0

ENGS = ("sync", "tensor", "scalar", "vector", "gpsimd")
_DBG = os.environ.get("KDBG", "")


class Sem:
    def __init__(self, nc, name):
        self.h = nc.semaphore(name).__enter__()
        self.n = 0

    def inc(self, k):
        self.n += k
        return self.n


def build_nc():
    nc = bass.Bass("TRN2", target_bir_lowering=False)

    x_h = nc.dram_tensor("x", [S, DM], F32, kind="ExternalInput").ap()
    wqkv_h = nc.dram_tensor("wqkv", [DM, FQ], F32, kind="ExternalInput").ap()
    wout_h = nc.dram_tensor("wout", [DM, DM], F32, kind="ExternalInput").ap()
    gw_h = nc.dram_tensor("gate_w", [NE, DM], F32, kind="ExternalInput").ap()
    gb_h = nc.dram_tensor("gate_b", [1, NE], F32, kind="ExternalInput").ap()
    w1_h = nc.dram_tensor("w1", [NE, HF, DM], F32, kind="ExternalInput").ap()
    b1_h = nc.dram_tensor("b1", [NE, HF], F32, kind="ExternalInput").ap()
    w2_h = nc.dram_tensor("w2", [NE, DM, HF], F32, kind="ExternalInput").ap()
    b2_h = nc.dram_tensor("b2", [NE, DM], F32, kind="ExternalInput").ap()
    w3_h = nc.dram_tensor("w3", [NE, HF, DM], F32, kind="ExternalInput").ap()
    b3_h = nc.dram_tensor("b3", [NE, HF], F32, kind="ExternalInput").ap()
    ws1_h = nc.dram_tensor("ws1", [HSH, DM], F32, kind="ExternalInput").ap()
    bs1_h = nc.dram_tensor("bs1", [1, HSH], F32, kind="ExternalInput").ap()
    ws2_h = nc.dram_tensor("ws2", [DM, HSH], F32, kind="ExternalInput").ap()
    bs2_h = nc.dram_tensor("bs2", [1, DM], F32, kind="ExternalInput").ap()
    ws3_h = nc.dram_tensor("ws3", [HSH, DM], F32, kind="ExternalInput").ap()
    bs3_h = nc.dram_tensor("bs3", [1, HSH], F32, kind="ExternalInput").ap()
    mask_h = nc.dram_tensor("mask", [128, 4 * 512], F32, kind="ExternalInput").ap()
    eye_h = nc.dram_tensor("eye", [128, 128], F32, kind="ExternalInput").ap()
    out_h = nc.dram_tensor("out", [S, DM], F32, kind="ExternalOutput").ap()

    dbg_out = {}
    if _DBG:
        for nm, shp, dt in (("dqs", [NH, 128, S], BF16), ("dks", [NH, 128, S], BF16),
                            ("dvs", [NH, S, 128], BF16), ("dos", [NH, 128, S], BF16),
                            ("dxln", [S, DM], F32), ("dxr", [S, DM], F32),
                        ("dosb", [128, 8192], F32), ("dwbs", [128, 8192], F32),
                        ("dcw", [128, 64], F32), ("dhs", [128, 11264], F32),
                        ("dys", [128, 8192], F32), ("dcwb", [128, 512], F32)):
            dbg_out[nm] = nc.dram_tensor(nm, shp, dt, kind="ExternalOutput").ap()

    pkqkv = nc.dram_tensor("pkqkv", [NFC * 128 * DM], BF16, kind="Internal").ap()
    pkwout = nc.dram_tensor("pkwout", [4 * 128 * 16 * 512], BF16, kind="Internal").ap()
    pk1 = nc.dram_tensor("pk1", [NE * NHC * 128 * DM], BF16, kind="Internal").ap()
    pk3 = nc.dram_tensor("pk3", [NE * NHC * 128 * DM], BF16, kind="Internal").ap()
    pk2 = nc.dram_tensor("pk2", [NE * 4 * 128 * NHC * 512], BF16, kind="Internal").ap()
    pks1 = nc.dram_tensor("pks1", [NSC * 128 * DM], BF16, kind="Internal").ap()
    pks3 = nc.dram_tensor("pks3", [NSC * 128 * DM], BF16, kind="Internal").ap()
    pks2 = nc.dram_tensor("pks2", [4 * 128 * NSC * 512], BF16, kind="Internal").ap()
    qscr = nc.dram_tensor("qscr", [NH, 128, S], BF16, kind="Internal").ap()
    kscr = nc.dram_tensor("kscr", [NH, 128, S], BF16, kind="Internal").ap()
    vscr = nc.dram_tensor("vscr", [NH, S, 128], BF16, kind="Internal").ap()
    oscr = nc.dram_tensor("oscr", [NH, 128, S], BF16, kind="Internal").ap()
    xlnscr = nc.dram_tensor("xlnscr", [S, DM], F32, kind="Internal").ap()

    pkqkv_v = pkqkv.rearrange("(fc p c) -> fc p c", fc=NFC, p=128, c=DM)
    pkwout_v = pkwout.rearrange("(ds p c) -> ds p c", ds=4, p=128, c=16 * 512)
    pk1_v = pk1.rearrange("(e hc p c) -> e hc p c", e=NE, hc=NHC, p=128, c=DM)
    pk3_v = pk3.rearrange("(e hc p c) -> e hc p c", e=NE, hc=NHC, p=128, c=DM)
    pk2_v = pk2.rearrange("(e ds p c) -> e ds p c", e=NE, ds=4, p=128, c=NHC * 512)
    pks1_v = pks1.rearrange("(hc p c) -> hc p c", hc=NSC, p=128, c=DM)
    pks3_v = pks3.rearrange("(hc p c) -> hc p c", hc=NSC, p=128, c=DM)
    pks2_v = pks2.rearrange("(ds p c) -> ds p c", ds=4, p=128, c=NSC * 512)

    _keep = []

    def sbt(name, shape, dt):
        cm = nc.sbuf_tensor(name, shape, dt)
        _keep.append(cm)
        return cm.__enter__()

    maskB = sbt("maskB", [128, 4 * 512], BF16)
    gate_wT = sbt("gate_wT", [128, NDC * 16], BF16)
    gb_b = sbt("gb_b", [128, 16], F32)
    b1T = sbt("b1T", [128, NHC * 16], F32)
    b3T = sbt("b3T", [128, NHC * 16], F32)
    bs1T = sbt("bs1T", [128, NSC], F32)
    bs3T = sbt("bs3T", [128, NSC], F32)
    b2b = sbt("b2b", [16, DM], BF16)
    bs2b = sbt("bs2b", [1, DM], BF16)
    eyeF = sbt("eyeF", [128, 128], F32)
    ones_bc = sbt("ones_bc", [128, 1], BF16)
    ones_fr = sbt("ones_fr", [1, 128], F32)
    ones_br = sbt("ones_br", [1, 128], BF16)
    cneg = sbt("cneg", [128, 1], F32)
    ceps = sbt("ceps", [128, 1], F32)
    cwT_f = sbt("cwT_f", [16, 512], F32)
    cwT_b = sbt("cwT_b", [16, 512], BF16)
    cwrow = sbt("cwrow", [1, 512], F32)
    CW = sbt("CW", [128, 64], F32)
    s1c = sbt("s1c", [128, 16], F32)
    st_s1 = sbt("st_s1", [128, 1], F32)
    st_s2 = sbt("st_s2", [128, 1], F32)
    st_m = sbt("st_m", [128, 1], F32)
    st_t1 = sbt("st_t1", [128, 1], F32)
    st_m2 = sbt("st_m2", [128, 1], F32)
    st_var = sbt("st_var", [128, 1], F32)
    st_std = sbt("st_std", [128, 1], F32)
    st_r = sbt("st_r", [128, 1], F32)
    st_a = sbt("st_a", [128, 1], F32)
    g_nm = sbt("g_nm", [128, 1], F32)
    g_rinv = sbt("g_rinv", [128, 1], F32)
    g_mx = sbt("g_mx", [128, 1], F32)
    g_sum = sbt("g_sum", [128, 1], F32)
    g_probs = sbt("g_probs", [128, 16], F32)
    g_bias = sbt("g_bias", [128, 16], F32)
    g_mask = sbt("g_mask", [128, 16], F32)
    g_mrow = sbt("g_mrow", [128, 16], F32)
    gbst = sbt("gbst", [1, 16], F32)
    l_sb = sbt("l_sb", [1, 512], F32)
    l_inv = sbt("l_inv", [1, 512], F32)

    AR = 186368
    arena = sbt("arena", [128, AR], U8)

    def carve(off, nbytes, dt, rows=128):
        assert off + nbytes <= AR, f"arena overflow {off}+{nbytes}"
        return arena[0:rows, off:off + nbytes].bitcast(dt)

    XTM = carve(0, 32768, F32)
    YSB = carve(32768, 32768, F32)
    XLNFM = carve(65536, 16384, BF16)
    XLNB = carve(81920, 4096, BF16)
    HSB = carve(86016, 22528, BF16)
    SFT = [carve(108544 + i * 2048, 2048, F32) for i in range(2)]
    T3B = [carve(112640 + i * 2048, 2048, F32) for i in range(2)]
    CWBb = [carve(116736 + i * 2048, 2048, F32) for i in range(2)]
    W1S = [carve(120832 + i * 4096, 4096, BF16) for i in range(2)]
    W3S = [carve(129024 + i * 4096, 4096, BF16) for i in range(2)]
    WBS = [carve(137216 + i * 16384, 16384, BF16) for i in range(2)]
    OSB = carve(169984, 16384, BF16)
    XSTG = [carve(0, 16384, BF16), carve(16384, 16384, BF16)]
    XTT = [carve(32768, 16384, BF16), carve(49152, 16384, BF16)]
    WQS = [carve(65536 + i * 4096, 4096, BF16) for i in range(3)]
    QKO = [carve(77824 + i * 1024, 1024, BF16) for i in range(3)]
    VTS = [carve(81920 + i * 1024, 1024, BF16) for i in range(2)]
    KHB = [carve(i * 8192, 8192, BF16) for i in range(2)]
    QHB = [carve(16384 + i * 8192, 8192, BF16) for i in range(2)]
    VTB = [carve(32768 + i * 8192, 8192, BF16) for i in range(2)]
    PTB = [carve(49152 + i * 1024, 1024, BF16) for i in range(8)]
    OBF = [carve(57344 + i * 1024, 1024, BF16) for i in range(2)]
    LVB = [carve(59392 + i * 2048, 2048, F32) for i in range(2)]
    PQ = [carve(110592 + i * 12288, 12288, BF16) for i in range(2)]
    PH = [carve(86016 + i * 6144, 6144, BF16) for i in range(2)]
    PHT = [carve(98304 + i * 6144, 6144, BF16) for i in range(2)]
    GWS = carve(135168, 8192, F32, rows=16)
    B1S = carve(143360, 5632, F32, rows=16)
    BS1S = carve(148992, 11264, F32, rows=1)

    PSB = []
    for i in range(8):
        cm = nc.psum_tensor(f"psb{i}", [128, 512], F32)
        _keep.append(cm)
        PSB.append(cm.__enter__())

    sG = Sem(nc, "sG")
    sG1 = Sem(nc, "sG1")
    sG2 = Sem(nc, "sG2")
    sW = Sem(nc, "sW")
    sW2 = Sem(nc, "sW2")
    sW5 = Sem(nc, "sW5")
    sW6 = Sem(nc, "sW6")
    sT = Sem(nc, "sT")
    sA = Sem(nc, "sA")
    sV = Sem(nc, "sV")
    sSt = Sem(nc, "sSt")
    allsems = [sG, sG1, sG2, sW, sW2, sW5, sW6, sT, sA, sV, sSt]

    prog = {e: [] for e in ENGS}

    def em(eng, fn):
        prog[eng].append(fn)

    def wt(eng, sem, val):
        if val > 0:
            em(eng, lambda e, h=sem.h, v=val: e.wait_ge(h, v))

    def barrier(sems=None):
        snap = [(s, s.n) for s in (sems if sems is not None else allsems)]
        for eng in ENGS:
            for s, v in snap:
                wt(eng, s, v)

    def dma(eng, sem, dst, src):
        t = sem.inc(16)
        em(eng, lambda e, d=dst, s=src, h=sem.h: e.dma_start(d, s).then_inc(h, 16))
        return t

    def dmaT(eng, sem, dst, src):
        t = sem.inc(16)
        em(eng, lambda e, d=dst, s=src, h=sem.h:
           e.dma_start(d, s, transpose=True).then_inc(h, 16))
        return t

    def blocked(apv, tile):
        return apv.rearrange("p (c f) -> p c f", f=tile)

    def mm(bank, lhsT, rhs, start, stop, tick=False, psl=None):
        out = PSB[bank][:, :] if psl is None else PSB[bank][psl[0], psl[1]]
        if tick:
            t = sT.inc(1)
            em("tensor", lambda e, o=out, l=lhsT, r=rhs, st=start, sp=stop:
               e.matmul(o, l, r, start=st, stop=sp).then_inc(sT.h, 1))
            return t
        em("tensor", lambda e, o=out, l=lhsT, r=rhs, st=start, sp=stop:
           e.matmul(o, l, r, start=st, stop=sp))
        return None

    dma("gpsimd", sG, maskB[:, :], mask_h)
    t_eye = dma("gpsimd", sG, eyeF[:, :], eye_h)
    dma("gpsimd", sG, b2b[:, :], b2_h)
    dma("gpsimd", sG, bs2b[:, :], bs2_h)
    t_gw = dma("gpsimd", sG, GWS, gw_h)
    t_gb = dma("gpsimd", sG, gbst[0:1, :], gb_h)
    em("gpsimd", lambda e: e.memset(ones_bc[:, :], 1.0))
    em("gpsimd", lambda e: e.memset(ones_br[:, :], 1.0).then_inc(sG.h, 1))
    sG.inc(1)
    em("vector", lambda e: e.memset(ones_fr[:, :], 1.0))
    em("vector", lambda e: e.memset(cneg[:, :], EXPSH))
    em("vector", lambda e: e.memset(ceps[:, :], 1e-5).then_inc(sV.h, 1))
    sV.inc(1)
    t_ones_v = sV.n

    wt("tensor", sG, t_gb)
    wt("tensor", sV, t_ones_v)
    tt = sT.inc(1)
    em("tensor", lambda e: e.matmul(PSB[5][:, 0:16], ones_fr[0:1, :], gbst[0:1, :],
                                    start=True, stop=True).then_inc(sT.h, 1))
    wt("scalar", sT, tt)
    t_ev = sA.inc(1)
    em("scalar", lambda e: e.copy(gb_b[:, :], PSB[5][:, 0:16]).then_inc(sA.h, 1))

    wt("tensor", sG, t_gw)
    wt("tensor", sG, t_eye)
    for dc in range(NDC):
        wt("tensor", sA, t_ev)
        tt = sT.inc(1)
        em("tensor", lambda e, dc=dc: e.transpose(
            PSB[7][:, 0:16], GWS[0:16, dc * 128:(dc + 1) * 128],
            eyeF[0:16, 0:16]).then_inc(sT.h, 1))
        wt("scalar", sT, tt)
        t_ev = sA.inc(1)
        em("scalar", lambda e, dc=dc: e.copy(
            gate_wT[:, dc * 16:(dc + 1) * 16], PSB[7][:, 0:16]).then_inc(sA.h, 1))

    def bias_mat(src_h, dstT):
        nonlocal t_ev
        wt("gpsimd", sT, sT.n)
        t_l = dma("gpsimd", sG, B1S, src_h)
        wt("tensor", sG, t_l)
        for hc in range(NHC):
            wt("tensor", sA, t_ev)
            tt = sT.inc(1)
            em("tensor", lambda e, hc=hc: e.transpose(
                PSB[7][:, 0:16], B1S[0:16, hc * 128:(hc + 1) * 128],
                eyeF[0:16, 0:16]).then_inc(sT.h, 1))
            wt("scalar", sT, tt)
            t_ev = sA.inc(1)
            em("scalar", lambda e, hc=hc, d=dstT: e.copy(
                d[:, hc * 16:(hc + 1) * 16], PSB[7][:, 0:16]).then_inc(sA.h, 1))

    def bias_vec(src_h, dstT):
        nonlocal t_ev
        wt("gpsimd", sT, sT.n)
        t_l = dma("gpsimd", sG, BS1S, src_h)
        wt("tensor", sG, t_l)
        for hc in range(NSC):
            wt("tensor", sA, t_ev)
            tt = sT.inc(1)
            em("tensor", lambda e, hc=hc: e.transpose(
                PSB[7][:, 0:1], BS1S[0:1, hc * 128:(hc + 1) * 128],
                eyeF[0:1, 0:1]).then_inc(sT.h, 1))
            wt("scalar", sT, tt)
            t_ev = sA.inc(1)
            em("scalar", lambda e, hc=hc, d=dstT: e.copy(
                d[:, hc:hc + 1], PSB[7][:, 0:1]).then_inc(sA.h, 1))

    bias_mat(b1_h, b1T)
    bias_mat(b3_h, b3T)
    bias_vec(bs1_h, bs1T)
    bias_vec(bs3_h, bs3T)

    wr_t = {}
    for dc in range(NDC):
        sl = dc % 2
        if dc >= 2:
            wt("gpsimd", sW, wr_t[dc - 2])
        t_l = dma("gpsimd", sG, PQ[sl][:, :], wqkv_h[dc * 128:(dc + 1) * 128, :])
        wt("sync", sG, t_l)
        wr_t[dc] = dma(
            "sync", sW,
            pkqkv.rearrange("(fc p c) -> p fc c", fc=NFC, p=128, c=DM)
                 [:, :, dc * 128:(dc + 1) * 128],
            PQ[sl][:, :].rearrange("p (fc c) -> p fc c", c=128))

    psteps = []
    pstate = {"i": 0, "ph": {}, "pht": {}}
    pk_tick = {}

    def _pk_step(src_ap, ncols, dst_ap, strided, transpose, key):
        def go():
            i = pstate["i"]
            sl = i % 2
            if i - 2 in pstate["ph"]:
                sem_, v_ = pstate["ph"][i - 2]
                wt("gpsimd", sem_, v_)
            t_l = dma("gpsimd", sG, PH[sl][:, 0:ncols], src_ap)
            if transpose:
                wt("sync", sG, t_l)
                if i - 2 in pstate["pht"]:
                    wt("sync", sG, pstate["pht"][i - 2])
                t_tr = dmaT("sync", sW,
                            blocked(PHT[sl][:, 0:ncols], 128),
                            blocked(PH[sl][:, 0:ncols], 128))
                pstate["ph"][i] = (sW, t_tr)
                wt("gpsimd", sW, t_tr)
                if strided:
                    t_wr = dma("gpsimd", sG, dst_ap,
                               PHT[sl][:, 0:ncols].rearrange(
                                   "p (hc c) -> p hc c", c=128))
                else:
                    t_wr = dma("gpsimd", sG, dst_ap, PHT[sl][:, 0:ncols])
                pstate["pht"][i] = t_wr
            else:
                wt("gpsimd", sG, t_l)
                t_wr = dma("gpsimd", sG, dst_ap, PH[sl][:, 0:ncols])
                pstate["ph"][i] = (sG, t_wr)
            pk_tick[key] = t_wr
            pstate["i"] = i + 1
        psteps.append(go)

    for dc in range(NDC):
        _pk_step(wout_h[dc * 128:(dc + 1) * 128, :], 2048,
                 pkwout.rearrange("(ds p c) -> p ds c", ds=4, p=128, c=16 * 512)
                       [:, :, dc * 512:(dc + 1) * 512]
                 if False else None, False, False, ("wo", dc))
    psteps.clear()
    pstate = {"i": 0, "ph": {}, "pht": {}}

    def _pk_wout(dc):
        def go():
            i = pstate["i"]
            sl = i % 2
            if i - 2 in pstate["ph"]:
                sem_, v_ = pstate["ph"][i - 2]
                wt("gpsimd", sem_, v_)
            t_l = dma("gpsimd", sG, PH[sl][:, 0:2048],
                      wout_h[dc * 128:(dc + 1) * 128, :])
            wt("gpsimd", sG, t_l)
            t_wr = dma("gpsimd", sG,
                       pkwout.rearrange("(ds p c) -> p ds c", ds=4, p=128,
                                        c=16 * 512)[:, :, dc * 512:(dc + 1) * 512],
                       PH[sl][:, 0:2048].rearrange("p (ds c) -> p ds c", c=512))
            pstate["ph"][i] = (sG, t_wr)
            pk_tick[("wo", dc)] = t_wr
            pstate["i"] = i + 1
        psteps.append(go)

    for dc in range(NDC):
        _pk_wout(dc)
    for e in range(NE):
        for hc in range(NHC):
            _pk_step(w1_h[e, hc * 128:(hc + 1) * 128, :], DM, pk1_v[e, hc],
                     False, True, ("w1", e, hc))
        for hc in range(NHC):
            _pk_step(w3_h[e, hc * 128:(hc + 1) * 128, :], DM, pk3_v[e, hc],
                     False, True, ("w3", e, hc))
        for dc in range(NDC):
            ds, q = dc // 4, dc % 4
            _pk_step(w2_h[e, dc * 128:(dc + 1) * 128, :], HF,
                     pk2_v[e, ds].rearrange("p (hc c) -> p hc c", c=512)
                                 [:, :, q * 128:(q + 1) * 128], True, True,
                     ("w2", e, dc))
    for hc in range(NSC):
        _pk_step(ws1_h[hc * 128:(hc + 1) * 128, :], DM, pks1_v[hc],
                 False, True, ("ws1", hc))
    for hc in range(NSC):
        _pk_step(ws3_h[hc * 128:(hc + 1) * 128, :], DM, pks3_v[hc],
                 False, True, ("ws3", hc))
    for dc in range(NDC):
        ds, q = dc // 4, dc % 4
        _pk_step(ws2_h[dc * 128:(dc + 1) * 128, :], HSH,
                 pks2_v[ds].rearrange("p (hc c) -> p hc c", c=512)
                           [:, :, q * 128:(q + 1) * 128], True, True,
                 ("ws2", dc))

    pump_pos = {"n": 0}

    def pump(k):
        for _ in range(k):
            if pump_pos["n"] < len(psteps):
                psteps[pump_pos["n"]]()
                pump_pos["n"] += 1

    CORE_SEMS = [sG1, sG2, sW2, sW5, sW6, sT, sA, sV, sSt]
    barrier()

    xt_done = {}
    mm_last_ot = {}
    mmg_t, ev_t, st_t = {}, {}, {}
    vts_st = {}
    gi = 0
    for ot in range(NOT):
        sl = ot % 2
        if ot >= 2:
            wt("gpsimd", sW2, xt_done[ot - 2])
        xl = []
        for sub in range(4):
            xl.append(dma("gpsimd", sG1,
                          XSTG[sl][:, sub * 2048:(sub + 1) * 2048],
                          x_h[ot * 512 + sub * 128: ot * 512 + (sub + 1) * 128, :]))
        if ot >= 2:
            wt("sync", sT, mm_last_ot[ot - 2])
        for sub in range(4):
            wt("sync", sG1, xl[sub])
            tick = dmaT(
                "sync", sW2,
                XTT[sl][:, :].rearrange("p (dc t) -> p dc t", t=512)
                             [:, :, sub * 128:(sub + 1) * 128],
                XSTG[sl][:, sub * 2048:(sub + 1) * 2048]
                    .rearrange("p (dc c) -> p dc c", c=128))
        xt_done[ot] = tick
        wt("tensor", sW2, xt_done[ot])
        for fc in range(NFC):
            wsl = gi % 3
            if gi >= 3:
                wt("sync", sT, mmg_t[gi - 3])
            t_slab = dma("sync", sW2, WQS[wsl][:, :], pkqkv_v[fc])
            wt("tensor", sW2, t_slab)
            if gi >= 2:
                wt("tensor", sA, ev_t[gi - 2])
            bank = gi % 2
            for dc in range(NDC):
                mmg_t[gi] = mm(bank, WQS[wsl][:, dc * 128:(dc + 1) * 128],
                               XTT[sl][:, dc * 512:(dc + 1) * 512],
                               dc == 0, dc == NDC - 1, tick=(dc == NDC - 1))
            wt("scalar", sT, mmg_t[gi])
            if gi >= 3:
                sem_q = sG1 if (gi - 3) % NFC < 32 else sW2
                wt("scalar", sem_q, st_t[gi - 3])
            ev_t[gi] = sA.inc(1)
            if fc < 16:
                em("scalar", lambda e, b=bank, o=QKO[wsl]:
                   e.mul(o[:, :], PSB[b][:, :], QSCALE).then_inc(sA.h, 1))
            else:
                em("scalar", lambda e, b=bank, o=QKO[wsl]:
                   e.copy(o[:, :], PSB[b][:, :]).then_inc(sA.h, 1))
            if fc < 32:
                wt("gpsimd", sA, ev_t[gi])
                scr = qscr if fc < 16 else kscr
                st_t[gi] = dma("gpsimd", sG1,
                               scr[fc % 16][:, ot * 512:(ot + 1) * 512],
                               QKO[wsl][:, :])
            else:
                vs = gi % 2
                wt("sync", sA, ev_t[gi])
                if gi >= 2:
                    wt("sync", sG1, vts_st.get(gi - 2, 0))
                t_vt = dmaT("sync", sW2,
                            VTS[vs][:, :].rearrange("p (tc c) -> p tc c", c=128),
                            QKO[wsl][:, :].rearrange("p (tc c) -> p tc c", c=128))
                st_t[gi] = t_vt
                wt("gpsimd", sW2, t_vt)
                vts_st[gi] = dma(
                    "gpsimd", sG1,
                    vscr[fc % 16][ot * 512:(ot + 1) * 512, :]
                        .rearrange("(tc p) c -> p tc c", p=128),
                    VTS[vs][:, :].rearrange("p (tc c) -> p tc c", c=128))
            gi += 1
        mm_last_ot[ot] = mmg_t[gi - 1]

    if _DBG == "p1":
        barrier()
        for nm, scr in (("dqs", qscr), ("dks", kscr), ("dvs", vscr)):
            dma("gpsimd", sSt, dbg_out[nm], scr)
        wt("gpsimd", sSt, sSt.n)
        return nc, prog

    barrier(CORE_SEMS)

    ld_t = {}
    h_consumed = {}
    pt_cons = {}
    exp_rdy = {}
    gk = 0
    oev_t = 0
    lvb_cons = {0: 0, 1: 0}
    obf_cons = {0: 0, 1: 0}
    for h in range(NH):
        sl = h % 2
        if h >= 2:
            wt("scalar", sT, h_consumed[h - 2])
        dma("scalar", sW2, KHB[sl][:, :], kscr[h])
        dma("scalar", sW2, QHB[sl][:, :], qscr[h])
        ld_t[h] = dma("scalar", sW2,
                      VTB[sl][:, :].rearrange("p (tc c) -> p tc c", c=128),
                      vscr[h].rearrange("(tc p) c -> p tc c", p=128))
        wt("tensor", sW2, ld_t[h])
        for qt in range(NOT):
            nkk = (qt + 1) * 4
            obank = 2 + qt % 2
            if oev_t:
                wt("tensor", sV, oev_t)

            def emit_S(kk, gk):
                if gk >= 2:
                    wt("tensor", exp_rdy[gk - 2][0], exp_rdy[gk - 2][1])
                smm = sT.inc(1)
                bank = gk % 2
                em("tensor", lambda e, b=bank, kk=kk, sl=sl, qt=qt:
                   e.matmul(PSB[b][:, :], KHB[sl][:, kk * 128:(kk + 1) * 128],
                            QHB[sl][:, qt * 512:(qt + 1) * 512],
                            start=True, stop=True).then_inc(sT.h, 1))
                slot = gk % 8
                if slot in pt_cons:
                    wt("scalar", sT, pt_cons[slot])
                wt("scalar", sT, smm)
                ta = sA.inc(1)
                em("scalar", lambda e, b=bank, sl2=slot:
                   e.activation(PTB[sl2][:, :], PSB[b][:, :], AF.Exp,
                                bias=cneg[:, :]).then_inc(sA.h, 1))
                exp_rdy[gk] = (sA, ta)
                if kk >= nkk - 4:
                    kc = kk % 4
                    wt("vector", sA, ta)
                    tv = sV.inc(1)
                    em("vector", lambda e, sl2=slot, kc=kc:
                       e.tensor_mul(PTB[sl2][:, :], PTB[sl2][:, :],
                                    maskB[:, kc * 512:(kc + 1) * 512]).then_inc(sV.h, 1))
                    exp_rdy[gk] = (sV, tv)

            def emit_PV(kk, gk):
                slot = gk % 8
                wt("tensor", exp_rdy[gk][0], exp_rdy[gk][1])
                em("tensor", lambda e, sl=sl, kk=kk, ob=obank, slot=slot,
                   first=(kk == 0), last=(kk == nkk - 1):
                   e.matmul(PSB[ob][:, :], VTB[sl][:, kk * 128:(kk + 1) * 128],
                            PTB[slot][:, :], start=first, stop=last))
                t2 = sT.inc(1)
                em("tensor", lambda e, kk=kk, slot=slot,
                   first=(kk == 0), last=(kk == nkk - 1):
                   e.matmul(PSB[4][0:1, :], ones_bc[:, 0:1],
                            PTB[slot][:, :], start=first, stop=last).then_inc(sT.h, 1))
                pt_cons[slot] = t2

            for kk in range(nkk):
                emit_S(kk, gk)
                if kk >= 1:
                    emit_PV(kk - 1, gk - 1)
                gk += 1
            emit_PV(nkk - 1, gk - 1)
            pv_done = sT.n
            wt("vector", sT, pv_done)
            tv = sV.inc(1)
            em("vector", lambda e: e.tensor_copy(l_sb[0:1, :],
                                                 PSB[4][0:1, :]).then_inc(sV.h, 1))
            wt("vector", sV, tv)
            tv2 = sV.inc(1)
            em("vector", lambda e: e.reciprocal(l_inv[0:1, :],
                                                l_sb[0:1, :]).then_inc(sV.h, 1))
            wt("tensor", sV, tv2)
            tb = sT.inc(1)
            em("tensor", lambda e: e.matmul(PSB[5][:, :], ones_fr[0:1, :],
                                            l_inv[0:1, :], start=True,
                                            stop=True).then_inc(sT.h, 1))
            wt("scalar", sT, tb)
            if lvb_cons[qt % 2]:
                wt("scalar", sV, lvb_cons[qt % 2])
            ta = sA.inc(1)
            em("scalar", lambda e, d=LVB[qt % 2]:
               e.copy(d[:, :], PSB[5][:, :]).then_inc(sA.h, 1))
            wt("vector", sA, ta)
            if obf_cons[qt % 2]:
                wt("vector", sG2, obf_cons[qt % 2])
            oev_t = lvb_cons[qt % 2] = sV.inc(1)
            em("vector", lambda e, ob=obank, d=OBF[qt % 2], lv=LVB[qt % 2]:
               e.tensor_mul(d[:, :], PSB[ob][:, :], lv[:, :]).then_inc(sV.h, 1))
            wt("gpsimd", sV, oev_t)
            obf_cons[qt % 2] = dma("gpsimd", sG2,
                                   oscr[h][:, qt * 512:(qt + 1) * 512],
                                   OBF[qt % 2][:, :])
            pump(6)
        h_consumed[h] = sT.n

    if _DBG == "p2":
        barrier()
        dma("gpsimd", sSt, dbg_out["dos"], oscr)
        wt("gpsimd", sSt, sSt.n)
        return nc, prog

    pump(len(psteps))
    pack_done_g = sG.n
    pack_done_w = sW.n
    barrier(CORE_SEMS)

    def small_chain(sem, ops):
        last = None
        for op in ops:
            if last is not None:
                wt("vector", sem, last)
            last = sem.inc(1)
            em("vector", lambda e, op=op, h=sem.h: op(e).then_inc(h, 1))
        return last

    def layer_norm(src_view, sq_view, out_view, rdy_t):
        wt("scalar", sV, rdy_t)
        ta = sA.inc(1)
        em("scalar", lambda e, s=src_view, q=sq_view:
           e.activation(q, s, AF.Square, accum_out=st_s2[:, :]).then_inc(sA.h, 1))
        wt("vector", sA, ta)
        t1 = sV.inc(1)
        em("vector", lambda e: e.tensor_scalar_mul(st_m[:, :], st_s1[:, :], 1.0 / DM))
        em("vector", lambda e: e.tensor_scalar_mul(st_t1[:, :], st_s2[:, :],
                                                   1.0 / DM).then_inc(sV.h, 1))
        wt("vector", sV, t1)
        t2 = sV.inc(1)
        em("vector", lambda e: e.tensor_mul(st_m2[:, :], st_m[:, :],
                                            st_m[:, :]).then_inc(sV.h, 1))
        wt("vector", sV, t2)
        t3 = sV.inc(1)
        em("vector", lambda e: e.tensor_sub(st_var[:, :], st_t1[:, :],
                                            st_m2[:, :]).then_inc(sV.h, 1))
        wt("scalar", sV, t3)
        t4 = sA.inc(1)
        em("scalar", lambda e: e.activation(st_std[:, :], st_var[:, :], AF.Sqrt,
                                            bias=ceps[:, :]).then_inc(sA.h, 1))
        wt("vector", sA, t4)
        t5 = sV.inc(1)
        em("vector", lambda e: e.reciprocal(st_r[:, :], st_std[:, :]).then_inc(sV.h, 1))
        wt("vector", sV, t5)
        t6 = sV.inc(1)
        em("vector", lambda e: e.tensor_mul(st_a[:, :], st_m[:, :],
                                            st_r[:, :]).then_inc(sV.h, 1))
        wt("vector", sV, t6)
        t7 = sV.inc(1)
        em("vector", lambda e, s=src_view, o=out_view:
           e.tensor_scalar(o, s, st_r[:, :], st_a[:, :], ALU.mult,
                           ALU.subtract).then_inc(sV.h, 1))
        return t7

    wbs_cons = {0: 0, 1: 0}
    w1s_cons = {0: 0, 1: 0}
    w3s_cons = {0: 0, 1: 0}
    cwb_cons = {0: 0, 1: 0}
    xlnfm_cons = 0

    for ot in range(NOT):
        sW5x = sW5 if ot < 4 else sW6
        xload = []
        for tsub in range(4):
            xload.append(dma("gpsimd", sG2, XTM[:, tsub * 2048:(tsub + 1) * 2048],
                             x_h[ot * 512 + tsub * 128: ot * 512 + (tsub + 1) * 128, :]))
        for ts_ in range(4):
            t_osb = dma("gpsimd", sG2,
                        OSB[:, ts_ * 2048:(ts_ + 1) * 2048]
                            .rearrange("p (hh c) -> p hh c", c=128),
                        oscr.rearrange("hh p q -> p hh q")
                            [:, :, ot * 512 + ts_ * 128: ot * 512 + (ts_ + 1) * 128])
        if _DBG == "p3o" and ot == 0:
            wt("gpsimd", sG2, t_osb)
            dma("gpsimd", sSt, dbg_out["dosb"], OSB[:, :])
        ev3 = {}
        for ds in range(4):
            wsl = ds % 2
            wt("sync", sT, wbs_cons[wsl])
            if ot == 0:
                wt("sync", sG, pk_tick[("wo", 15)])
            t_w = dma("sync", sW5x, WBS[wsl][:, :], pkwout_v[ds])
            if _DBG == "p3o" and ot == 0 and ds == 0:
                wt("gpsimd", sW5x, t_w)
                dma("gpsimd", sSt, dbg_out["dwbs"], WBS[0][:, :])
            wt("tensor", sW5x, t_w)
            wt("tensor", sG2, t_osb)
            for tsub in range(4):
                k = ds * 4 + tsub
                bank = k % 2
                if k >= 2:
                    wt("tensor", sV, ev3[k - 2])
                hh_order = (list(reversed(range(16)))
                            if os.environ.get("KREV") else list(range(16)))
                for i_, hh in enumerate(hh_order):
                    tmm = mm(bank,
                             OSB[:, (tsub * 16 + hh) * 128:(tsub * 16 + hh + 1) * 128],
                             WBS[wsl][:, hh * 512:(hh + 1) * 512],
                             i_ == 0, i_ == 15, tick=(i_ == 15))
                wt("vector", sT, tmm)
                wt("vector", sG2, xload[tsub])
                ev3[k] = sV.inc(1)
                em("vector", lambda e, b=bank, tsub=tsub, ds=ds:
                   e.scalar_tensor_tensor(
                       XTM[:, tsub * 2048 + ds * 512: tsub * 2048 + (ds + 1) * 512],
                       PSB[b][:, :], 1.0,
                       XTM[:, tsub * 2048 + ds * 512: tsub * 2048 + (ds + 1) * 512],
                       ALU.mult, ALU.add,
                       accum_out=s1c[:, tsub * 4 + ds: tsub * 4 + ds + 1]).then_inc(sV.h, 1))
            wbs_cons[wsl] = sT.n
        ln_sp = {}
        fm_tick = {}
        g_ps_cons = (None, 0)
        for tsub in range(4):
            if _DBG in ("p3", "p3o"):
                wt("gpsimd", sV, ev3[12 + tsub])
                dma("gpsimd", sG2,
                    dbg_out["dxr"][ot * 512 + tsub * 128:
                                   ot * 512 + (tsub + 1) * 128, :],
                    XTM[:, tsub * 2048:(tsub + 1) * 2048])
            wt("vector", sV, ev3[12 + tsub])
            ts1 = sV.inc(1)
            em("vector", lambda e, tsub=tsub: e.reduce_sum(
                st_s1[:, :], s1c[:, tsub * 4:(tsub + 1) * 4],
                axis=AX.X).then_inc(sV.h, 1))
            xv = XTM[:, tsub * 2048:(tsub + 1) * 2048]
            sqv = YSB[:, tsub * 2048:(tsub + 1) * 2048]
            t_ln = layer_norm(xv, sqv, sqv, ts1)
            wt("gpsimd", sV, t_ln)
            ln_sp[tsub] = dma("gpsimd", sG2,
                              xlnscr[ot * 512 + tsub * 128:
                                     ot * 512 + (tsub + 1) * 128, :], sqv)
            wt("scalar", sV, t_ln)
            tc = sA.inc(1)
            em("scalar", lambda e, s=sqv: e.copy(XLNB[:, :], s).then_inc(sA.h, 1))
            wt("sync", sA, tc)
            if ot > 0 and tsub == 0:
                wt("sync", sT, xlnfm_cons)
            fm_tick[tsub] = dmaT(
                "sync", sW5x,
                XLNFM[:, :].rearrange("p (dc t) -> p dc t", t=512)
                           [:, :, tsub * 128:(tsub + 1) * 128],
                blocked(XLNB[:, :], 128))
            if _DBG == "p3o":
                continue
            wt("tensor", sW5x, fm_tick[tsub])
            if g_ps_cons[0] is not None:
                wt("tensor", g_ps_cons[0], g_ps_cons[1])
            for dc in range(NDC):
                tg = mm(2, XLNFM[:, dc * 512 + tsub * 128: dc * 512 + (tsub + 1) * 128],
                        gate_wT[:, dc * 16:(dc + 1) * 16],
                        dc == 0, dc == NDC - 1, tick=(dc == NDC - 1),
                        psl=(slice(0, 128), slice(0, 16)))
            wt("vector", sT, tg)
            tv = sV.inc(1)
            em("vector", lambda e: e.reduce_max(g_nm[:, :], PSB[2][:, 0:16],
                                                axis=AX.X, negate=True).then_inc(sV.h, 1))
            wt("scalar", sV, tv)
            ta = sA.inc(1)
            em("scalar", lambda e: e.activation(
                g_probs[:, :], PSB[2][:, 0:16], AF.Exp, bias=g_nm[:, :],
                accum_out=g_sum[:, :]).then_inc(sA.h, 1))
            g_ps_cons = (sA, ta)
            wt("vector", sA, ta)
            ops = [
                lambda e: e.reciprocal(g_rinv[:, :], g_sum[:, :]),
                lambda e: e.tensor_scalar_mul(g_probs[:, :], g_probs[:, :],
                                              g_rinv[:, :]),
                lambda e: e.tensor_add(g_bias[:, :], g_probs[:, :], gb_b[:, :]),
                lambda e: e.memset(g_mask[:, :], 0.0),
            ]
            for _r in range(6):
                ops += [
                    lambda e: e.reduce_max(g_mx[:, :], g_bias[:, :], axis=AX.X),
                    lambda e: e.tensor_scalar(g_mrow[:, :], g_bias[:, :],
                                              g_mx[:, :], None, ALU.is_ge),
                    lambda e: e.tensor_add(g_mask[:, :], g_mask[:, :], g_mrow[:, :]),
                    lambda e: e.scalar_tensor_tensor(g_bias[:, :], g_mrow[:, :],
                                                     -1e9, g_bias[:, :],
                                                     ALU.mult, ALU.add),
                ]
            ops.append(lambda e, tsub=tsub: e.tensor_mul(
                CW[:, tsub * 16:(tsub + 1) * 16], g_probs[:, :], g_mask[:, :]))
            tcw = small_chain(sV, ops)
            wt("tensor", sV, tcw)
            if tsub > 0 or ot > 0:
                wt("tensor", sA, cwt_ev)
            ttr = sT.inc(1)
            em("tensor", lambda e, tsub=tsub: e.transpose(
                PSB[7][0:16, 0:128], CW[:, tsub * 16:(tsub + 1) * 16],
                eyeF[:, :]).then_inc(sT.h, 1))
            wt("scalar", sT, ttr)
            em("scalar", lambda e, tsub=tsub: e.copy(
                cwT_f[:, tsub * 128:(tsub + 1) * 128], PSB[7][0:16, 0:128]))
            cwt_ev = sA.inc(1)
            em("scalar", lambda e, tsub=tsub: e.copy(
                cwT_b[:, tsub * 128:(tsub + 1) * 128],
                PSB[7][0:16, 0:128]).then_inc(sA.h, 1))
        gate_done = sA.n

        if _DBG == "p3o":
            barrier()
            if ot == NOT - 1:
                dma("gpsimd", sSt, dbg_out["dxln"], xlnscr)
                wt("gpsimd", sSt, sSt.n)
                return nc, prog
            continue

        if _DBG == "p3" and ot == NOT - 1:
            barrier()
            dma("gpsimd", sSt, dbg_out["dxln"], xlnscr)
            wt("gpsimd", sSt, sSt.n)
            return nc, prog

        wt("vector", sG2, ln_sp[3])
        wt("vector", sA, gate_done)
        wt("tensor", sW5x, fm_tick[3])
        if ot == 0:
            wt("scalar", sG, pack_done_g)
            wt("scalar", sW, pack_done_w)
            wt("vector", sG, pack_done_g)
            wt("vector", sW, pack_done_w)
        cwrow_cons = 0
        y_ev = {}
        tcb = 0
        for e_i in range(NE + 1):
            shared = e_i == NE
            nhc = NSC if shared else NHC
            csl = e_i % 2
            hev = {}
            for hc in range(nhc):
                wsl = hc % 2
                wt("sync", sT, w1s_cons[wsl])
                if ot == 0:
                    wt("sync", sG, pk_tick[("ws1", hc) if shared else ("w1", e_i, hc)])
                    wt("sync", sG, pk_tick[("ws3", hc) if shared else ("w3", e_i, hc)])
                t_w1 = dma("sync", sW5x, W1S[wsl][:, :],
                           pks1_v[hc] if shared else pk1_v[e_i, hc])
                wt("sync", sT, w3s_cons[wsl])
                t_w3 = dma("sync", sW5x, W3S[wsl][:, :],
                           pks3_v[hc] if shared else pk3_v[e_i, hc])
                b1k = hc % 2
                b3k = 2 + hc % 2
                wt("tensor", sW5x, t_w3)
                if hc >= 2:
                    wt("tensor", sA, hev[hc - 2][0])
                    wt("tensor", sV, hev[hc - 2][1])
                for dc in range(NDC):
                    mm(b1k, W1S[wsl][:, dc * 128:(dc + 1) * 128],
                       XLNFM[:, dc * 512:(dc + 1) * 512], dc == 0, dc == NDC - 1)
                for dc in range(NDC):
                    tmm = mm(b3k, W3S[wsl][:, dc * 128:(dc + 1) * 128],
                             XLNFM[:, dc * 512:(dc + 1) * 512], dc == 0,
                             dc == NDC - 1, tick=(dc == NDC - 1))
                w1s_cons[wsl] = w3s_cons[wsl] = tmm
                if hc == 0 and not shared:
                    wt("gpsimd", sA, gate_done)
                    if cwrow_cons:
                        wt("gpsimd", sT, cwrow_cons)
                    t_row = dma("gpsimd", sG2, cwrow[0:1, :], cwT_f[e_i:e_i + 1, :])
                    wt("tensor", sG2, t_row)
                    if cwb_cons[csl]:
                        wt("tensor", sV, cwb_cons[csl])
                    cwrow_cons = sT.inc(1)
                    em("tensor", lambda e: e.matmul(
                        PSB[4][:, :], ones_fr[0:1, :], cwrow[0:1, :],
                        start=True, stop=True).then_inc(sT.h, 1))
                    wt("scalar", sT, cwrow_cons)
                    tcb = sA.inc(1)
                    em("scalar", lambda e, d=CWBb[csl]:
                       e.copy(d[:, :], PSB[4][:, :]).then_inc(sA.h, 1))
                wt("scalar", sT, tmm)
                bcol = (bs1T[:, hc:hc + 1] if shared
                        else b1T[:, hc * 16 + e_i: hc * 16 + e_i + 1])
                ta = sA.inc(1)
                em("scalar", lambda e, b=b1k, d=SFT[hc % 2], bc=bcol:
                   e.activation(d[:, :], PSB[b][:, :], AF.Silu,
                                bias=bc).then_inc(sA.h, 1))
                wt("vector", sT, tmm)
                b3col = (bs3T[:, hc:hc + 1] if shared
                         else b3T[:, hc * 16 + e_i: hc * 16 + e_i + 1])
                if shared:
                    tv = sV.inc(1)
                    em("vector", lambda e, b=b3k, d=T3B[hc % 2], bc=b3col:
                       e.tensor_scalar_add(d[:, :], PSB[b][:, :],
                                           bc).then_inc(sV.h, 1))
                else:
                    wt("vector", sA, tcb)
                    tv = sV.inc(1)
                    em("vector", lambda e, b=b3k, d=T3B[hc % 2], bc=b3col,
                       cb=CWBb[csl]:
                       e.scalar_tensor_tensor(d[:, :], PSB[b][:, :], bc, cb[:, :],
                                              ALU.add, ALU.mult).then_inc(sV.h, 1))
                wt("vector", sA, ta)
                wt("vector", sV, tv)
                tv2 = sV.inc(1)
                em("vector", lambda e, d=SFT[hc % 2], t3=T3B[hc % 2], hc=hc:
                   e.tensor_mul(HSB[:, hc * 512:(hc + 1) * 512], d[:, :],
                                t3[:, :]).then_inc(sV.h, 1))
                hev[hc] = (ta, tv2)
            h_done = sV.n
            cwb_cons[csl] = sV.n
            if _DBG == "p5" and ot == 0 and e_i == 0:
                wt("gpsimd", sV, h_done)
                wt("gpsimd", sA, tcb)
                dma("gpsimd", sSt, dbg_out["dhs"], HSB[:, :])
                dma("gpsimd", sSt, dbg_out["dcwb"], CWBb[csl][:, :])
                dma("gpsimd", sSt, dbg_out["dcw"], CW[:, :])
            for ds in range(4):
                wsl = ds % 2
                if shared:
                    wt("sync", sT, wbs_cons[0])
                    wt("sync", sT, wbs_cons[1])
                    if ot == 0:
                        wt("sync", sG, pk_tick[("ws2", 15)])
                    dma("sync", sW5x, WBS[0][:, :], pks2_v[ds][:, 0:8192])
                    t_w = dma("sync", sW5x, WBS[1][:, 0:3072],
                              pks2_v[ds][:, 8192:11264])
                else:
                    wt("sync", sT, wbs_cons[wsl])
                    if ot == 0:
                        wt("sync", sG, pk_tick[("w2", e_i, 15)])
                    t_w = dma("sync", sW5x, WBS[wsl][:, 0:nhc * 512],
                              pk2_v[e_i, ds])
                wt("tensor", sW5x, t_w)
                wt("tensor", sV, h_done)
                for tsub in range(4):
                    k = ds * 4 + tsub
                    bank = 5 + k % 2
                    prev = y_ev.get((e_i, k - 2))
                    if prev is None and e_i > 0:
                        prev = y_ev.get((e_i - 1, 14 + k))
                    if prev:
                        wt("tensor", sV, prev)
                    for hc in range(nhc):
                        last = (hc == nhc - 1) and not shared
                        if shared:
                            wsrc = (WBS[0][:, hc * 512:(hc + 1) * 512] if hc < 16
                                    else WBS[1][:, (hc - 16) * 512:(hc - 15) * 512])
                        else:
                            wsrc = WBS[wsl][:, hc * 512:(hc + 1) * 512]
                        tmm = mm(bank,
                                 HSB[:, hc * 512 + tsub * 128:
                                     hc * 512 + (tsub + 1) * 128],
                                 wsrc,
                                 hc == 0, last, tick=last)
                    if shared:
                        mm(bank, cwT_b[:, tsub * 128:(tsub + 1) * 128],
                           b2b[0:16, ds * 512:(ds + 1) * 512], False, False)
                        tmm = mm(bank, ones_br[0:1, :],
                                 bs2b[0:1, ds * 512:(ds + 1) * 512],
                                 False, True, tick=True)
                    wt("vector", sT, tmm)
                    yv = YSB[:, tsub * 2048 + ds * 512: tsub * 2048 + (ds + 1) * 512]
                    y_ev[(e_i, k)] = sV.inc(1)
                    if e_i == 0:
                        em("vector", lambda e, b=bank, yv=yv:
                           e.tensor_copy(yv, PSB[b][:, :]).then_inc(sV.h, 1))
                    else:
                        em("vector", lambda e, b=bank, yv=yv:
                           e.tensor_add(yv, yv, PSB[b][:, :]).then_inc(sV.h, 1))
                if shared:
                    wbs_cons[0] = wbs_cons[1] = sT.n
                else:
                    wbs_cons[wsl] = sT.n
            if _DBG == "p5" and ot == 0 and e_i == 0:
                for kk_ in range(16):
                    wt("gpsimd", sV, y_ev[(0, kk_)])
                dma("gpsimd", sSt, dbg_out["dys"], YSB[:, :])
                wt("gpsimd", sSt, sSt.n)
                return nc, prog
        xlnfm_cons = sT.n

        for tsub in range(4):
            t_xl = dma("gpsimd", sG2, XTM[:, tsub * 2048:(tsub + 1) * 2048],
                       xlnscr[ot * 512 + tsub * 128: ot * 512 + (tsub + 1) * 128, :])
            for ds in range(4):
                wt("vector", sV, y_ev[(NE, ds * 4 + tsub)])
            wt("vector", sG2, t_xl)
            yv = YSB[:, tsub * 2048:(tsub + 1) * 2048]
            xv = XTM[:, tsub * 2048:(tsub + 1) * 2048]
            t_add = sV.inc(1)
            em("vector", lambda e, yv=yv, xv=xv:
               e.scalar_tensor_tensor(yv, xv, 1.0, yv, ALU.mult, ALU.add,
                                      accum_out=st_s1[:, :]).then_inc(sV.h, 1))
            t_ln = layer_norm(yv, xv, xv, t_add)
            wt("gpsimd", sV, t_ln)
            dma("gpsimd", sSt,
                out_h[ot * 512 + tsub * 128: ot * 512 + (tsub + 1) * 128, :], xv)
        barrier(CORE_SEMS)

    wt("gpsimd", sSt, sSt.n)
    return nc, prog


def _run_block(nc, prog):
    with nc.Block() as block:
        @block.sync
        def _(e):
            for fn in prog["sync"]:
                fn(e)

        @block.tensor
        def _(e):
            for fn in prog["tensor"]:
                fn(e)

        @block.scalar
        def _(e):
            for fn in prog["scalar"]:
                fn(e)

        @block.vector
        def _(e):
            for fn in prog["vector"]:
                fn(e)

        @block.gpsimd
        def _(e):
            for fn in prog["gpsimd"]:
                fn(e)
    return nc


def _make_consts():
    mask = np.zeros((128, 4 * 512), np.float32)
    kp = np.arange(128)[:, None]
    qf = np.arange(512)[None, :]
    for c in range(4):
        mask[:, c * 512:(c + 1) * 512] = (qf >= c * 128 + kp).astype(np.float32)
    eye = np.eye(128, dtype=np.float32)
    return mask, eye


def build():
    nc, prog = build_nc()
    _run_block(nc, prog)
    return nc


def make_in_maps(inputs):
    mask, eye = _make_consts()
    f = np.float32
    shared = {
        "wqkv": np.ascontiguousarray(np.asarray(inputs["L0_attn_w_qkv"], f)),
        "wout": np.ascontiguousarray(np.asarray(inputs["L0_attn_w_out"], f)),
        "gate_w": np.ascontiguousarray(np.asarray(inputs["L0_ffn_gate_w"], f)),
        "gate_b": np.ascontiguousarray(np.asarray(inputs["L0_ffn_gate_b"], f)).reshape(1, NE),
        "w1": np.ascontiguousarray(np.asarray(inputs["L0_ffn_w1"], f)),
        "b1": np.ascontiguousarray(np.asarray(inputs["L0_ffn_b1"], f)),
        "w2": np.ascontiguousarray(np.asarray(inputs["L0_ffn_w2"], f)),
        "b2": np.ascontiguousarray(np.asarray(inputs["L0_ffn_b2"], f)),
        "w3": np.ascontiguousarray(np.asarray(inputs["L0_ffn_w3"], f)),
        "b3": np.ascontiguousarray(np.asarray(inputs["L0_ffn_b3"], f)),
        "ws1": np.ascontiguousarray(np.asarray(inputs["L0_ffn_ws1"], f)),
        "bs1": np.ascontiguousarray(np.asarray(inputs["L0_ffn_bs1"], f)).reshape(1, HSH),
        "ws2": np.ascontiguousarray(np.asarray(inputs["L0_ffn_ws2"], f)),
        "bs2": np.ascontiguousarray(np.asarray(inputs["L0_ffn_bs2"], f)).reshape(1, DM),
        "ws3": np.ascontiguousarray(np.asarray(inputs["L0_ffn_ws3"], f)),
        "bs3": np.ascontiguousarray(np.asarray(inputs["L0_ffn_bs3"], f)).reshape(1, HSH),
        "mask": mask,
        "eye": eye,
    }
    x = np.asarray(inputs["L0_attn_x"], f)
    return [dict(shared, x=np.ascontiguousarray(x[i])) for i in range(8)]


def kernel(**inputs):
    nc = build()
    in_maps = make_in_maps(inputs)
    res = run_bass_kernel_spmd(nc, in_maps, core_ids=list(range(8)))
    return np.stack([np.asarray(r["out"]) for r in res.results], axis=0)


# version 08 (final)
# speedup vs baseline: 1.1585x; 1.1585x over previous
import os
import numpy as np

import concourse.bass as bass
import concourse.mybir as mybir
from concourse.bass_utils import run_bass_kernel_spmd

F32 = mybir.dt.float32
BF16 = mybir.dt.bfloat16
U8 = mybir.dt.uint8
AF = mybir.ActivationFunctionType
ALU = mybir.AluOpType
AX = mybir.AxisListType

S = 4096
DM = 2048
FQ = 6144
NH = 16
NE = 16
HF = 1408
HSH = 2816
NDC = DM // 128
NFC = FQ // 128
NHC = HF // 128
NSC = HSH // 128
NOT = S // 512
QSCALE = float(1.0 / np.sqrt(np.float32(128.0)))
EXPSH = -10.0

ENGS = ("sync", "tensor", "scalar", "vector", "gpsimd")
_DBG = os.environ.get("KDBG", "")


class Sem:
    def __init__(self, nc, name):
        self.h = nc.semaphore(name).__enter__()
        self.n = 0

    def inc(self, k):
        self.n += k
        return self.n


def build_nc():
    nc = bass.Bass("TRN2", target_bir_lowering=False)

    x_h = nc.dram_tensor("x", [S, DM], F32, kind="ExternalInput").ap()
    wqkv_h = nc.dram_tensor("wqkv", [DM, FQ], F32, kind="ExternalInput").ap()
    wout_h = nc.dram_tensor("wout", [DM, DM], F32, kind="ExternalInput").ap()
    gw_h = nc.dram_tensor("gate_w", [NE, DM], F32, kind="ExternalInput").ap()
    gb_h = nc.dram_tensor("gate_b", [1, NE], F32, kind="ExternalInput").ap()
    w1_h = nc.dram_tensor("w1", [NE, HF, DM], F32, kind="ExternalInput").ap()
    b1_h = nc.dram_tensor("b1", [NE, HF], F32, kind="ExternalInput").ap()
    w2_h = nc.dram_tensor("w2", [NE, DM, HF], F32, kind="ExternalInput").ap()
    b2_h = nc.dram_tensor("b2", [NE, DM], F32, kind="ExternalInput").ap()
    w3_h = nc.dram_tensor("w3", [NE, HF, DM], F32, kind="ExternalInput").ap()
    b3_h = nc.dram_tensor("b3", [NE, HF], F32, kind="ExternalInput").ap()
    ws1_h = nc.dram_tensor("ws1", [HSH, DM], F32, kind="ExternalInput").ap()
    bs1_h = nc.dram_tensor("bs1", [1, HSH], F32, kind="ExternalInput").ap()
    ws2_h = nc.dram_tensor("ws2", [DM, HSH], F32, kind="ExternalInput").ap()
    bs2_h = nc.dram_tensor("bs2", [1, DM], F32, kind="ExternalInput").ap()
    ws3_h = nc.dram_tensor("ws3", [HSH, DM], F32, kind="ExternalInput").ap()
    bs3_h = nc.dram_tensor("bs3", [1, HSH], F32, kind="ExternalInput").ap()
    mask_h = nc.dram_tensor("mask", [128, 4 * 512], F32, kind="ExternalInput").ap()
    eye_h = nc.dram_tensor("eye", [128, 128], F32, kind="ExternalInput").ap()
    out_h = nc.dram_tensor("out", [S, DM], F32, kind="ExternalOutput").ap()

    dbg_out = {}
    if _DBG:
        for nm, shp, dt in (("dqs", [NH, 128, S], BF16), ("dks", [NH, 128, S], BF16),
                            ("dvs", [NH, S, 128], BF16), ("dos", [NH, 128, S], BF16),
                            ("dxln", [S, DM], F32), ("dxr", [S, DM], F32),
                        ("dosb", [128, 8192], F32), ("dwbs", [128, 8192], F32),
                        ("dcw", [128, 64], F32), ("dhs", [128, 11264], F32),
                        ("dys", [128, 8192], F32), ("dcwb", [128, 512], F32)):
            dbg_out[nm] = nc.dram_tensor(nm, shp, dt, kind="ExternalOutput").ap()

    pkqkv = nc.dram_tensor("pkqkv", [NFC * 128 * DM], BF16, kind="Internal").ap()
    pkwout = nc.dram_tensor("pkwout", [4 * 128 * 16 * 512], BF16, kind="Internal").ap()
    pk1 = nc.dram_tensor("pk1", [NE * NHC * 128 * DM], BF16, kind="Internal").ap()
    pk3 = nc.dram_tensor("pk3", [NE * NHC * 128 * DM], BF16, kind="Internal").ap()
    pk2 = nc.dram_tensor("pk2", [NE * 4 * 128 * NHC * 512], BF16, kind="Internal").ap()
    pks1 = nc.dram_tensor("pks1", [NSC * 128 * DM], BF16, kind="Internal").ap()
    pks3 = nc.dram_tensor("pks3", [NSC * 128 * DM], BF16, kind="Internal").ap()
    pks2 = nc.dram_tensor("pks2", [4 * 128 * NSC * 512], BF16, kind="Internal").ap()
    qscr = nc.dram_tensor("qscr", [NH, 128, S], BF16, kind="Internal").ap()
    kscr = nc.dram_tensor("kscr", [NH, 128, S], BF16, kind="Internal").ap()
    vscr = nc.dram_tensor("vscr", [NH, S, 128], BF16, kind="Internal").ap()
    oscr = nc.dram_tensor("oscr", [NH, 128, S], BF16, kind="Internal").ap()
    xlnscr = nc.dram_tensor("xlnscr", [S, DM], F32, kind="Internal").ap()

    pkqkv_v = pkqkv.rearrange("(fc p c) -> fc p c", fc=NFC, p=128, c=DM)
    pkwout_v = pkwout.rearrange("(ds p c) -> ds p c", ds=4, p=128, c=16 * 512)
    pk1_v = pk1.rearrange("(e hc p c) -> e hc p c", e=NE, hc=NHC, p=128, c=DM)
    pk3_v = pk3.rearrange("(e hc p c) -> e hc p c", e=NE, hc=NHC, p=128, c=DM)
    pk2_v = pk2.rearrange("(e ds p c) -> e ds p c", e=NE, ds=4, p=128, c=NHC * 512)
    pks1_v = pks1.rearrange("(hc p c) -> hc p c", hc=NSC, p=128, c=DM)
    pks3_v = pks3.rearrange("(hc p c) -> hc p c", hc=NSC, p=128, c=DM)
    pks2_v = pks2.rearrange("(ds p c) -> ds p c", ds=4, p=128, c=NSC * 512)

    _keep = []

    def sbt(name, shape, dt):
        cm = nc.sbuf_tensor(name, shape, dt)
        _keep.append(cm)
        return cm.__enter__()

    maskB = sbt("maskB", [128, 4 * 512], BF16)
    gate_wT = sbt("gate_wT", [128, NDC * 16], BF16)
    gb_b = sbt("gb_b", [128, 16], F32)
    b1T = sbt("b1T", [128, NHC * 16], F32)
    b3T = sbt("b3T", [128, NHC * 16], F32)
    bs1T = sbt("bs1T", [128, NSC], F32)
    bs3T = sbt("bs3T", [128, NSC], F32)
    b2b = sbt("b2b", [16, DM], BF16)
    bs2b = sbt("bs2b", [1, DM], BF16)
    eyeF = sbt("eyeF", [128, 128], F32)
    ones_bc = sbt("ones_bc", [128, 1], BF16)
    ones_fr = sbt("ones_fr", [1, 128], F32)
    ones_br = sbt("ones_br", [1, 128], BF16)
    cneg = sbt("cneg", [128, 1], F32)
    ceps = sbt("ceps", [128, 1], F32)
    cwT_f = sbt("cwT_f", [16, 512], F32)
    cwT_b = sbt("cwT_b", [16, 512], BF16)
    cwrow = sbt("cwrow", [1, 512], F32)
    CW = sbt("CW", [128, 64], F32)
    s1c = sbt("s1c", [128, 16], F32)
    st_s1 = sbt("st_s1", [128, 1], F32)
    st_s2 = sbt("st_s2", [128, 1], F32)
    st_m = sbt("st_m", [128, 1], F32)
    st_t1 = sbt("st_t1", [128, 1], F32)
    st_m2 = sbt("st_m2", [128, 1], F32)
    st_var = sbt("st_var", [128, 1], F32)
    st_std = sbt("st_std", [128, 1], F32)
    st_r = sbt("st_r", [128, 1], F32)
    st_a = sbt("st_a", [128, 1], F32)
    g_nm = sbt("g_nm", [128, 1], F32)
    g_rinv = sbt("g_rinv", [128, 1], F32)
    g_mx = sbt("g_mx", [128, 1], F32)
    g_sum = sbt("g_sum", [128, 1], F32)
    g_probs = sbt("g_probs", [128, 16], F32)
    g_bias = sbt("g_bias", [128, 16], F32)
    g_mask = sbt("g_mask", [128, 16], F32)
    g_mrow = sbt("g_mrow", [128, 16], F32)
    gbst = sbt("gbst", [1, 16], F32)
    l_sb = sbt("l_sb", [1, 512], F32)
    l_inv = sbt("l_inv", [1, 512], F32)

    AR = 186368
    arena = sbt("arena", [128, AR], U8)

    def carve(off, nbytes, dt, rows=128):
        assert off + nbytes <= AR, f"arena overflow {off}+{nbytes}"
        return arena[0:rows, off:off + nbytes].bitcast(dt)

    XTM = carve(0, 32768, F32)
    YSB = carve(32768, 32768, F32)
    XLNFM = carve(65536, 16384, BF16)
    XLNB = carve(81920, 4096, BF16)
    HSB = carve(86016, 22528, BF16)
    SFT = [carve(108544 + i * 2048, 2048, F32) for i in range(2)]
    T3B = [carve(112640 + i * 2048, 2048, F32) for i in range(2)]
    CWBb = [carve(116736 + i * 2048, 2048, F32) for i in range(2)]
    W1S = [carve(120832 + i * 4096, 4096, BF16) for i in range(2)]
    W3S = [carve(129024 + i * 4096, 4096, BF16) for i in range(2)]
    WBS = [carve(137216 + i * 16384, 16384, BF16) for i in range(2)]
    OSB = carve(169984, 16384, BF16)
    XSTG = [carve(0, 16384, BF16), carve(16384, 16384, BF16)]
    XTT = [carve(32768, 16384, BF16), carve(49152, 16384, BF16)]
    WQS = [carve(110592 + i * 4096, 4096, BF16) for i in range(6)]
    QKO = [carve(65536 + i * 1024, 1024, BF16) for i in range(6)]
    VTS = [carve(71680 + i * 1024, 1024, BF16) for i in range(2)]
    KHB = [carve(i * 8192, 8192, BF16) for i in range(2)]
    QHB = [carve(16384 + i * 8192, 8192, BF16) for i in range(2)]
    VTB = [carve(32768 + i * 8192, 8192, BF16) for i in range(2)]
    PTB = [carve(49152 + i * 1024, 1024, BF16) for i in range(8)]
    OBF = [carve(57344 + i * 1024, 1024, BF16) for i in range(2)]
    LVB = [carve(59392 + i * 2048, 2048, F32) for i in range(2)]
    PQ = [carve(110592 + i * 12288, 12288, BF16) for i in range(2)]
    PH = [carve(86016 + i * 6144, 6144, BF16) for i in range(2)]
    PHT = [carve(98304 + i * 6144, 6144, BF16) for i in range(2)]
    GWS = carve(135168, 8192, F32, rows=16)
    B1S = carve(143360, 5632, F32, rows=16)
    BS1S = carve(148992, 11264, F32, rows=1)

    PSB = []
    for i in range(8):
        cm = nc.psum_tensor(f"psb{i}", [128, 512], F32)
        _keep.append(cm)
        PSB.append(cm.__enter__())

    sG = Sem(nc, "sG")
    sG1 = Sem(nc, "sG1")
    sG2 = Sem(nc, "sG2")
    sW = Sem(nc, "sW")
    sW2 = Sem(nc, "sW2")
    sW5 = Sem(nc, "sW5")
    sW6 = Sem(nc, "sW6")
    sT = Sem(nc, "sT")
    sA = Sem(nc, "sA")
    sV = Sem(nc, "sV")
    sSt = Sem(nc, "sSt")
    allsems = [sG, sG1, sG2, sW, sW2, sW5, sW6, sT, sA, sV, sSt]

    prog = {e: [] for e in ENGS}

    def em(eng, fn):
        prog[eng].append(fn)

    def wt(eng, sem, val):
        if val > 0:
            em(eng, lambda e, h=sem.h, v=val: e.wait_ge(h, v))

    def barrier(sems=None):
        snap = [(s, s.n) for s in (sems if sems is not None else allsems)]
        for eng in ENGS:
            for s, v in snap:
                wt(eng, s, v)

    def dma(eng, sem, dst, src):
        t = sem.inc(16)
        em(eng, lambda e, d=dst, s=src, h=sem.h: e.dma_start(d, s).then_inc(h, 16))
        return t

    def dmaT(eng, sem, dst, src):
        t = sem.inc(16)
        em(eng, lambda e, d=dst, s=src, h=sem.h:
           e.dma_start(d, s, transpose=True).then_inc(h, 16))
        return t

    def blocked(apv, tile):
        return apv.rearrange("p (c f) -> p c f", f=tile)

    def mm(bank, lhsT, rhs, start, stop, tick=False, psl=None):
        out = PSB[bank][:, :] if psl is None else PSB[bank][psl[0], psl[1]]
        if tick:
            t = sT.inc(1)
            em("tensor", lambda e, o=out, l=lhsT, r=rhs, st=start, sp=stop:
               e.matmul(o, l, r, start=st, stop=sp).then_inc(sT.h, 1))
            return t
        em("tensor", lambda e, o=out, l=lhsT, r=rhs, st=start, sp=stop:
           e.matmul(o, l, r, start=st, stop=sp))
        return None

    dma("gpsimd", sG, maskB[:, :], mask_h)
    t_eye = dma("gpsimd", sG, eyeF[:, :], eye_h)
    dma("gpsimd", sG, b2b[:, :], b2_h)
    dma("gpsimd", sG, bs2b[:, :], bs2_h)
    t_gw = dma("gpsimd", sG, GWS, gw_h)
    t_gb = dma("gpsimd", sG, gbst[0:1, :], gb_h)
    em("gpsimd", lambda e: e.memset(ones_bc[:, :], 1.0))
    em("gpsimd", lambda e: e.memset(ones_br[:, :], 1.0).then_inc(sG.h, 1))
    sG.inc(1)
    em("vector", lambda e: e.memset(ones_fr[:, :], 1.0))
    em("vector", lambda e: e.memset(cneg[:, :], EXPSH))
    em("vector", lambda e: e.memset(ceps[:, :], 1e-5).then_inc(sV.h, 1))
    sV.inc(1)
    t_ones_v = sV.n

    wt("tensor", sG, t_gb)
    wt("tensor", sV, t_ones_v)
    tt = sT.inc(1)
    em("tensor", lambda e: e.matmul(PSB[5][:, 0:16], ones_fr[0:1, :], gbst[0:1, :],
                                    start=True, stop=True).then_inc(sT.h, 1))
    wt("scalar", sT, tt)
    t_ev = sA.inc(1)
    em("scalar", lambda e: e.copy(gb_b[:, :], PSB[5][:, 0:16]).then_inc(sA.h, 1))

    wt("tensor", sG, t_gw)
    wt("tensor", sG, t_eye)
    for dc in range(NDC):
        wt("tensor", sA, t_ev)
        tt = sT.inc(1)
        em("tensor", lambda e, dc=dc: e.transpose(
            PSB[7][:, 0:16], GWS[0:16, dc * 128:(dc + 1) * 128],
            eyeF[0:16, 0:16]).then_inc(sT.h, 1))
        wt("scalar", sT, tt)
        t_ev = sA.inc(1)
        em("scalar", lambda e, dc=dc: e.copy(
            gate_wT[:, dc * 16:(dc + 1) * 16], PSB[7][:, 0:16]).then_inc(sA.h, 1))

    def bias_mat(src_h, dstT):
        nonlocal t_ev
        wt("gpsimd", sT, sT.n)
        t_l = dma("gpsimd", sG, B1S, src_h)
        wt("tensor", sG, t_l)
        for hc in range(NHC):
            wt("tensor", sA, t_ev)
            tt = sT.inc(1)
            em("tensor", lambda e, hc=hc: e.transpose(
                PSB[7][:, 0:16], B1S[0:16, hc * 128:(hc + 1) * 128],
                eyeF[0:16, 0:16]).then_inc(sT.h, 1))
            wt("scalar", sT, tt)
            t_ev = sA.inc(1)
            em("scalar", lambda e, hc=hc, d=dstT: e.copy(
                d[:, hc * 16:(hc + 1) * 16], PSB[7][:, 0:16]).then_inc(sA.h, 1))

    def bias_vec(src_h, dstT):
        nonlocal t_ev
        wt("gpsimd", sT, sT.n)
        t_l = dma("gpsimd", sG, BS1S, src_h)
        wt("tensor", sG, t_l)
        for hc in range(NSC):
            wt("tensor", sA, t_ev)
            tt = sT.inc(1)
            em("tensor", lambda e, hc=hc: e.transpose(
                PSB[7][:, 0:1], BS1S[0:1, hc * 128:(hc + 1) * 128],
                eyeF[0:1, 0:1]).then_inc(sT.h, 1))
            wt("scalar", sT, tt)
            t_ev = sA.inc(1)
            em("scalar", lambda e, hc=hc, d=dstT: e.copy(
                d[:, hc:hc + 1], PSB[7][:, 0:1]).then_inc(sA.h, 1))

    bias_mat(b1_h, b1T)
    bias_mat(b3_h, b3T)
    bias_vec(bs1_h, bs1T)
    bias_vec(bs3_h, bs3T)

    wr_t = {}
    for dc in range(NDC):
        sl = dc % 2
        if dc >= 2:
            wt("gpsimd", sW, wr_t[dc - 2])
        t_l = dma("gpsimd", sG, PQ[sl][:, :], wqkv_h[dc * 128:(dc + 1) * 128, :])
        wt("sync", sG, t_l)
        wr_t[dc] = dma(
            "sync", sW,
            pkqkv.rearrange("(fc p c) -> p fc c", fc=NFC, p=128, c=DM)
                 [:, :, dc * 128:(dc + 1) * 128],
            PQ[sl][:, :].rearrange("p (fc c) -> p fc c", c=128))

    psteps = []
    pstate = {"i": 0, "ph": {}, "pht": {}}
    pk_tick = {}

    def _pk_step(src_ap, ncols, dst_ap, strided, transpose, key):
        def go():
            i = pstate["i"]
            sl = i % 2
            if i - 2 in pstate["ph"]:
                sem_, v_ = pstate["ph"][i - 2]
                wt("gpsimd", sem_, v_)
            t_l = dma("gpsimd", sG, PH[sl][:, 0:ncols], src_ap)
            if transpose:
                wt("sync", sG, t_l)
                if i - 2 in pstate["pht"]:
                    wt("sync", sW, pstate["pht"][i - 2])
                t_tr = dmaT("sync", sW,
                            blocked(PHT[sl][:, 0:ncols], 128),
                            blocked(PH[sl][:, 0:ncols], 128))
                pstate["ph"][i] = (sW, t_tr)
                wt("sync", sW, t_tr)
                if strided:
                    t_wr = dma("sync", sW, dst_ap,
                               PHT[sl][:, 0:ncols].rearrange(
                                   "p (hc c) -> p hc c", c=128))
                else:
                    t_wr = dma("sync", sW, dst_ap, PHT[sl][:, 0:ncols])
                pstate["pht"][i] = t_wr
            else:
                wt("sync", sG, t_l)
                t_wr = dma("sync", sW, dst_ap, PH[sl][:, 0:ncols])
                pstate["ph"][i] = (sW, t_wr)
            pk_tick[key] = t_wr
            pstate["i"] = i + 1
        psteps.append(go)

    for dc in range(NDC):
        _pk_step(wout_h[dc * 128:(dc + 1) * 128, :], 2048,
                 pkwout.rearrange("(ds p c) -> p ds c", ds=4, p=128, c=16 * 512)
                       [:, :, dc * 512:(dc + 1) * 512]
                 if False else None, False, False, ("wo", dc))
    psteps.clear()
    pstate = {"i": 0, "ph": {}, "pht": {}}

    def _pk_wout(dc):
        def go():
            i = pstate["i"]
            sl = i % 2
            if i - 2 in pstate["ph"]:
                sem_, v_ = pstate["ph"][i - 2]
                wt("gpsimd", sem_, v_)
            t_l = dma("gpsimd", sG, PH[sl][:, 0:2048],
                      wout_h[dc * 128:(dc + 1) * 128, :])
            wt("sync", sG, t_l)
            t_wr = dma("sync", sW,
                       pkwout.rearrange("(ds p c) -> p ds c", ds=4, p=128,
                                        c=16 * 512)[:, :, dc * 512:(dc + 1) * 512],
                       PH[sl][:, 0:2048].rearrange("p (ds c) -> p ds c", c=512))
            pstate["ph"][i] = (sW, t_wr)
            pk_tick[("wo", dc)] = t_wr
            pstate["i"] = i + 1
        psteps.append(go)

    for dc in range(NDC):
        _pk_wout(dc)
    for e in range(NE):
        for hc in range(NHC):
            _pk_step(w1_h[e, hc * 128:(hc + 1) * 128, :], DM, pk1_v[e, hc],
                     False, True, ("w1", e, hc))
        for hc in range(NHC):
            _pk_step(w3_h[e, hc * 128:(hc + 1) * 128, :], DM, pk3_v[e, hc],
                     False, True, ("w3", e, hc))
        for dc in range(NDC):
            ds, q = dc // 4, dc % 4
            _pk_step(w2_h[e, dc * 128:(dc + 1) * 128, :], HF,
                     pk2_v[e, ds].rearrange("p (hc c) -> p hc c", c=512)
                                 [:, :, q * 128:(q + 1) * 128], True, True,
                     ("w2", e, dc))
    for hc in range(NSC):
        _pk_step(ws1_h[hc * 128:(hc + 1) * 128, :], DM, pks1_v[hc],
                 False, True, ("ws1", hc))
    for hc in range(NSC):
        _pk_step(ws3_h[hc * 128:(hc + 1) * 128, :], DM, pks3_v[hc],
                 False, True, ("ws3", hc))
    for dc in range(NDC):
        ds, q = dc // 4, dc % 4
        _pk_step(ws2_h[dc * 128:(dc + 1) * 128, :], HSH,
                 pks2_v[ds].rearrange("p (hc c) -> p hc c", c=512)
                           [:, :, q * 128:(q + 1) * 128], True, True,
                 ("ws2", dc))

    pump_pos = {"n": 0}

    def pump(k):
        for _ in range(k):
            if pump_pos["n"] < len(psteps):
                psteps[pump_pos["n"]]()
                pump_pos["n"] += 1

    CORE_SEMS = [sG1, sG2, sW2, sW5, sW6, sT, sA, sV, sSt]
    barrier()

    xt_done = {}
    mm_last_ot = {}
    mmg_t, ev_t, st_t, slab_t = {}, {}, {}, {}
    vts_st = {}
    gi = 0
    xl_t = {}

    def load_x(ot2):
        sl2 = ot2 % 2
        if ot2 >= 2:
            wt("gpsimd", sW2, xt_done[ot2 - 2])
        for sub in range(4):
            xl_t[ot2] = dma(
                "gpsimd", sG1, XSTG[sl2][:, sub * 2048:(sub + 1) * 2048],
                x_h[ot2 * 512 + sub * 128: ot2 * 512 + (sub + 1) * 128, :])

    load_x(0)
    for ot in range(NOT):
        sl = ot % 2
        if ot >= 2:
            wt("sync", sT, mm_last_ot[ot - 2])
        wt("sync", sG1, xl_t[ot])
        for sub in range(4):
            tick = dmaT(
                "sync", sW2,
                XTT[sl][:, :].rearrange("p (dc t) -> p dc t", t=512)
                             [:, :, sub * 128:(sub + 1) * 128],
                XSTG[sl][:, sub * 2048:(sub + 1) * 2048]
                    .rearrange("p (dc c) -> p dc c", c=128))
        xt_done[ot] = tick
        wt("tensor", sW2, xt_done[ot])

        def issue_slab(fc2, gi2):
            wsl2 = gi2 % 6
            if gi2 >= 6:
                wt("sync", sT, mmg_t[gi2 - 6])
            slab_t[gi2] = dma("sync", sW2, WQS[wsl2][:, :], pkqkv_v[fc2])

        for k3 in range(3):
            issue_slab(k3, gi + k3)
        for fc in range(NFC):
            if fc + 3 < NFC:
                issue_slab(fc + 3, gi + 3)
            wsl = gi % 6
            qsl = gi % 6
            wt("tensor", sW2, slab_t[gi])
            if gi >= 2:
                wt("tensor", sA, ev_t[gi - 2])
            bank = gi % 2
            for dc in range(NDC):
                mmg_t[gi] = mm(bank, WQS[wsl][:, dc * 128:(dc + 1) * 128],
                               XTT[sl][:, dc * 512:(dc + 1) * 512],
                               dc == 0, dc == NDC - 1, tick=(dc == NDC - 1))
            wt("scalar", sT, mmg_t[gi])
            if gi >= 6:
                sem_q = sG1 if (gi - 6) % NFC < 32 else sW2
                wt("scalar", sem_q, st_t[gi - 6])
            ev_t[gi] = sA.inc(1)
            if fc < 16:
                em("scalar", lambda e, b=bank, o=QKO[qsl]:
                   e.mul(o[:, :], PSB[b][:, :], QSCALE).then_inc(sA.h, 1))
            else:
                em("scalar", lambda e, b=bank, o=QKO[qsl]:
                   e.copy(o[:, :], PSB[b][:, :]).then_inc(sA.h, 1))
            if fc < 32:
                wt("gpsimd", sA, ev_t[gi])
                scr = qscr if fc < 16 else kscr
                st_t[gi] = dma("gpsimd", sG1,
                               scr[fc % 16][:, ot * 512:(ot + 1) * 512],
                               QKO[qsl][:, :])
            else:
                vs = gi % 2
                wt("sync", sA, ev_t[gi])
                if gi >= 2:
                    wt("sync", sG1, vts_st.get(gi - 2, 0))
                t_vt = dmaT("sync", sW2,
                            VTS[vs][:, :].rearrange("p (tc c) -> p tc c", c=128),
                            QKO[qsl][:, :].rearrange("p (tc c) -> p tc c", c=128))
                st_t[gi] = t_vt
                wt("gpsimd", sW2, t_vt)
                vts_st[gi] = dma(
                    "gpsimd", sG1,
                    vscr[fc % 16][ot * 512:(ot + 1) * 512, :]
                        .rearrange("(tc p) c -> p tc c", p=128),
                    VTS[vs][:, :].rearrange("p (tc c) -> p tc c", c=128))
            if fc == 24 and ot + 1 < NOT:
                load_x(ot + 1)
            if fc % 4 == 3:
                pump(1)
            gi += 1
        mm_last_ot[ot] = mmg_t[gi - 1]

    if _DBG == "p1":
        barrier()
        for nm, scr in (("dqs", qscr), ("dks", kscr), ("dvs", vscr)):
            dma("gpsimd", sSt, dbg_out[nm], scr)
        wt("gpsimd", sSt, sSt.n)
        return nc, prog

    barrier(CORE_SEMS)

    ld_t = {}
    h_consumed = {}
    pt_cons = {}
    exp_rdy = {}
    gk = 0
    oev_t = 0
    lvb_cons = {0: 0, 1: 0}
    obf_cons = {0: 0, 1: 0}
    for h in range(NH):
        sl = h % 2
        if h >= 2:
            wt("scalar", sT, h_consumed[h - 2])
        dma("scalar", sW2, KHB[sl][:, :], kscr[h])
        dma("scalar", sW2, QHB[sl][:, :], qscr[h])
        ld_t[h] = dma("scalar", sW2,
                      VTB[sl][:, :].rearrange("p (tc c) -> p tc c", c=128),
                      vscr[h].rearrange("(tc p) c -> p tc c", p=128))
        wt("tensor", sW2, ld_t[h])
        for qt in range(NOT):
            nkk = (qt + 1) * 4
            obank = 2 + qt % 2
            if oev_t:
                wt("tensor", sV, oev_t)

            def emit_S(kk, gk):
                if gk >= 2:
                    wt("tensor", exp_rdy[gk - 2][0], exp_rdy[gk - 2][1])
                smm = sT.inc(1)
                bank = gk % 2
                em("tensor", lambda e, b=bank, kk=kk, sl=sl, qt=qt:
                   e.matmul(PSB[b][:, :], KHB[sl][:, kk * 128:(kk + 1) * 128],
                            QHB[sl][:, qt * 512:(qt + 1) * 512],
                            start=True, stop=True).then_inc(sT.h, 1))
                slot = gk % 8
                if slot in pt_cons:
                    wt("scalar", sT, pt_cons[slot])
                wt("scalar", sT, smm)
                ta = sA.inc(1)
                em("scalar", lambda e, b=bank, sl2=slot:
                   e.activation(PTB[sl2][:, :], PSB[b][:, :], AF.Exp,
                                bias=cneg[:, :]).then_inc(sA.h, 1))
                exp_rdy[gk] = (sA, ta)
                if kk >= nkk - 4:
                    kc = kk % 4
                    wt("vector", sA, ta)
                    tv = sV.inc(1)
                    em("vector", lambda e, sl2=slot, kc=kc:
                       e.tensor_mul(PTB[sl2][:, :], PTB[sl2][:, :],
                                    maskB[:, kc * 512:(kc + 1) * 512]).then_inc(sV.h, 1))
                    exp_rdy[gk] = (sV, tv)

            def emit_PV(kk, gk):
                slot = gk % 8
                wt("tensor", exp_rdy[gk][0], exp_rdy[gk][1])
                em("tensor", lambda e, sl=sl, kk=kk, ob=obank, slot=slot,
                   first=(kk == 0), last=(kk == nkk - 1):
                   e.matmul(PSB[ob][:, :], VTB[sl][:, kk * 128:(kk + 1) * 128],
                            PTB[slot][:, :], start=first, stop=last))
                t2 = sT.inc(1)
                em("tensor", lambda e, kk=kk, slot=slot,
                   first=(kk == 0), last=(kk == nkk - 1):
                   e.matmul(PSB[4][0:1, :], ones_bc[:, 0:1],
                            PTB[slot][:, :], start=first, stop=last).then_inc(sT.h, 1))
                pt_cons[slot] = t2

            for kk in range(nkk):
                emit_S(kk, gk)
                if kk >= 1:
                    emit_PV(kk - 1, gk - 1)
                gk += 1
            emit_PV(nkk - 1, gk - 1)
            pv_done = sT.n
            wt("vector", sT, pv_done)
            tv = sV.inc(1)
            em("vector", lambda e: e.tensor_copy(l_sb[0:1, :],
                                                 PSB[4][0:1, :]).then_inc(sV.h, 1))
            wt("vector", sV, tv)
            tv2 = sV.inc(1)
            em("vector", lambda e: e.reciprocal(l_inv[0:1, :],
                                                l_sb[0:1, :]).then_inc(sV.h, 1))
            wt("tensor", sV, tv2)
            tb = sT.inc(1)
            em("tensor", lambda e: e.matmul(PSB[5][:, :], ones_fr[0:1, :],
                                            l_inv[0:1, :], start=True,
                                            stop=True).then_inc(sT.h, 1))
            wt("scalar", sT, tb)
            if lvb_cons[qt % 2]:
                wt("scalar", sV, lvb_cons[qt % 2])
            ta = sA.inc(1)
            em("scalar", lambda e, d=LVB[qt % 2]:
               e.copy(d[:, :], PSB[5][:, :]).then_inc(sA.h, 1))
            wt("vector", sA, ta)
            if obf_cons[qt % 2]:
                wt("vector", sG2, obf_cons[qt % 2])
            oev_t = lvb_cons[qt % 2] = sV.inc(1)
            em("vector", lambda e, ob=obank, d=OBF[qt % 2], lv=LVB[qt % 2]:
               e.tensor_mul(d[:, :], PSB[ob][:, :], lv[:, :]).then_inc(sV.h, 1))
            wt("scalar", sV, oev_t)
            obf_cons[qt % 2] = dma("scalar", sG2,
                                   oscr[h][:, qt * 512:(qt + 1) * 512],
                                   OBF[qt % 2][:, :])
            pump(6)
        h_consumed[h] = sT.n

    if _DBG == "p2":
        barrier()
        dma("gpsimd", sSt, dbg_out["dos"], oscr)
        wt("gpsimd", sSt, sSt.n)
        return nc, prog

    pump(len(psteps))
    pack_done_g = sG.n
    pack_done_w = sW.n
    barrier(CORE_SEMS)

    def small_chain(sem, ops):
        last = None
        for op in ops:
            if last is not None:
                wt("vector", sem, last)
            last = sem.inc(1)
            em("vector", lambda e, op=op, h=sem.h: op(e).then_inc(h, 1))
        return last

    def layer_norm(src_view, sq_view, out_view, rdy_t):
        wt("scalar", sV, rdy_t)
        ta = sA.inc(1)
        em("scalar", lambda e, s=src_view, q=sq_view:
           e.activation(q, s, AF.Square, accum_out=st_s2[:, :]).then_inc(sA.h, 1))
        wt("vector", sA, ta)
        t1 = sV.inc(1)
        em("vector", lambda e: e.tensor_scalar_mul(st_m[:, :], st_s1[:, :], 1.0 / DM))
        em("vector", lambda e: e.tensor_scalar_mul(st_t1[:, :], st_s2[:, :],
                                                   1.0 / DM).then_inc(sV.h, 1))
        wt("vector", sV, t1)
        t2 = sV.inc(1)
        em("vector", lambda e: e.tensor_mul(st_m2[:, :], st_m[:, :],
                                            st_m[:, :]).then_inc(sV.h, 1))
        wt("vector", sV, t2)
        t3 = sV.inc(1)
        em("vector", lambda e: e.tensor_sub(st_var[:, :], st_t1[:, :],
                                            st_m2[:, :]).then_inc(sV.h, 1))
        wt("scalar", sV, t3)
        t4 = sA.inc(1)
        em("scalar", lambda e: e.activation(st_std[:, :], st_var[:, :], AF.Sqrt,
                                            bias=ceps[:, :]).then_inc(sA.h, 1))
        wt("vector", sA, t4)
        t5 = sV.inc(1)
        em("vector", lambda e: e.reciprocal(st_r[:, :], st_std[:, :]).then_inc(sV.h, 1))
        wt("vector", sV, t5)
        t6 = sV.inc(1)
        em("vector", lambda e: e.tensor_mul(st_a[:, :], st_m[:, :],
                                            st_r[:, :]).then_inc(sV.h, 1))
        wt("vector", sV, t6)
        t7 = sV.inc(1)
        em("vector", lambda e, s=src_view, o=out_view:
           e.tensor_scalar(o, s, st_r[:, :], st_a[:, :], ALU.mult,
                           ALU.subtract).then_inc(sV.h, 1))
        return t7

    wbs_cons = {0: 0, 1: 0}
    w1s_cons = {0: 0, 1: 0}
    w3s_cons = {0: 0, 1: 0}
    cwb_cons = {0: 0, 1: 0}
    xlnfm_cons = 0

    for ot in range(NOT):
        sW5x = sW5 if ot < 4 else sW6
        xload = []
        for tsub in range(4):
            xload.append(dma("gpsimd", sG2, XTM[:, tsub * 2048:(tsub + 1) * 2048],
                             x_h[ot * 512 + tsub * 128: ot * 512 + (tsub + 1) * 128, :]))
        for ts_ in range(4):
            t_osb = dma("gpsimd", sG2,
                        OSB[:, ts_ * 2048:(ts_ + 1) * 2048]
                            .rearrange("p (hh c) -> p hh c", c=128),
                        oscr.rearrange("hh p q -> p hh q")
                            [:, :, ot * 512 + ts_ * 128: ot * 512 + (ts_ + 1) * 128])
        if _DBG == "p3o" and ot == 0:
            wt("gpsimd", sG2, t_osb)
            dma("gpsimd", sSt, dbg_out["dosb"], OSB[:, :])
        ev3 = {}
        for ds in range(4):
            wsl = ds % 2
            wt("sync", sT, wbs_cons[wsl])
            if ot == 0:
                wt("sync", sW, pk_tick[("wo", 15)])
            t_w = dma("sync", sW5x, WBS[wsl][:, :], pkwout_v[ds])
            if _DBG == "p3o" and ot == 0 and ds == 0:
                wt("gpsimd", sW5x, t_w)
                dma("gpsimd", sSt, dbg_out["dwbs"], WBS[0][:, :])
            wt("tensor", sW5x, t_w)
            wt("tensor", sG2, t_osb)
            for tsub in range(4):
                k = ds * 4 + tsub
                bank = k % 2
                if k >= 2:
                    wt("tensor", sV, ev3[k - 2])
                hh_order = (list(reversed(range(16)))
                            if os.environ.get("KREV") else list(range(16)))
                for i_, hh in enumerate(hh_order):
                    tmm = mm(bank,
                             OSB[:, (tsub * 16 + hh) * 128:(tsub * 16 + hh + 1) * 128],
                             WBS[wsl][:, hh * 512:(hh + 1) * 512],
                             i_ == 0, i_ == 15, tick=(i_ == 15))
                wt("vector", sT, tmm)
                wt("vector", sG2, xload[tsub])
                ev3[k] = sV.inc(1)
                em("vector", lambda e, b=bank, tsub=tsub, ds=ds:
                   e.scalar_tensor_tensor(
                       XTM[:, tsub * 2048 + ds * 512: tsub * 2048 + (ds + 1) * 512],
                       PSB[b][:, :], 1.0,
                       XTM[:, tsub * 2048 + ds * 512: tsub * 2048 + (ds + 1) * 512],
                       ALU.mult, ALU.add,
                       accum_out=s1c[:, tsub * 4 + ds: tsub * 4 + ds + 1]).then_inc(sV.h, 1))
            wbs_cons[wsl] = sT.n
        ln_sp = {}
        fm_tick = {}
        g_ps_cons = (None, 0)
        for tsub in range(4):
            if _DBG in ("p3", "p3o"):
                wt("gpsimd", sV, ev3[12 + tsub])
                dma("gpsimd", sG2,
                    dbg_out["dxr"][ot * 512 + tsub * 128:
                                   ot * 512 + (tsub + 1) * 128, :],
                    XTM[:, tsub * 2048:(tsub + 1) * 2048])
            wt("vector", sV, ev3[12 + tsub])
            ts1 = sV.inc(1)
            em("vector", lambda e, tsub=tsub: e.reduce_sum(
                st_s1[:, :], s1c[:, tsub * 4:(tsub + 1) * 4],
                axis=AX.X).then_inc(sV.h, 1))
            xv = XTM[:, tsub * 2048:(tsub + 1) * 2048]
            sqv = YSB[:, tsub * 2048:(tsub + 1) * 2048]
            t_ln = layer_norm(xv, sqv, sqv, ts1)
            wt("gpsimd", sV, t_ln)
            ln_sp[tsub] = dma("gpsimd", sG2,
                              xlnscr[ot * 512 + tsub * 128:
                                     ot * 512 + (tsub + 1) * 128, :], sqv)
            wt("scalar", sV, t_ln)
            tc = sA.inc(1)
            em("scalar", lambda e, s=sqv: e.copy(XLNB[:, :], s).then_inc(sA.h, 1))
            wt("sync", sA, tc)
            if ot > 0 and tsub == 0:
                wt("sync", sT, xlnfm_cons)
            fm_tick[tsub] = dmaT(
                "sync", sW5x,
                XLNFM[:, :].rearrange("p (dc t) -> p dc t", t=512)
                           [:, :, tsub * 128:(tsub + 1) * 128],
                blocked(XLNB[:, :], 128))
            if _DBG == "p3o":
                continue
            wt("tensor", sW5x, fm_tick[tsub])
            if g_ps_cons[0] is not None:
                wt("tensor", g_ps_cons[0], g_ps_cons[1])
            for dc in range(NDC):
                tg = mm(2, XLNFM[:, dc * 512 + tsub * 128: dc * 512 + (tsub + 1) * 128],
                        gate_wT[:, dc * 16:(dc + 1) * 16],
                        dc == 0, dc == NDC - 1, tick=(dc == NDC - 1),
                        psl=(slice(0, 128), slice(0, 16)))
            wt("vector", sT, tg)
            tv = sV.inc(1)
            em("vector", lambda e: e.reduce_max(g_nm[:, :], PSB[2][:, 0:16],
                                                axis=AX.X, negate=True).then_inc(sV.h, 1))
            wt("scalar", sV, tv)
            ta = sA.inc(1)
            em("scalar", lambda e: e.activation(
                g_probs[:, :], PSB[2][:, 0:16], AF.Exp, bias=g_nm[:, :],
                accum_out=g_sum[:, :]).then_inc(sA.h, 1))
            g_ps_cons = (sA, ta)
            wt("vector", sA, ta)
            ops = [
                lambda e: e.reciprocal(g_rinv[:, :], g_sum[:, :]),
                lambda e: e.tensor_scalar_mul(g_probs[:, :], g_probs[:, :],
                                              g_rinv[:, :]),
                lambda e: e.tensor_add(g_bias[:, :], g_probs[:, :], gb_b[:, :]),
                lambda e: e.memset(g_mask[:, :], 0.0),
            ]
            for _r in range(6):
                ops += [
                    lambda e: e.reduce_max(g_mx[:, :], g_bias[:, :], axis=AX.X),
                    lambda e: e.tensor_scalar(g_mrow[:, :], g_bias[:, :],
                                              g_mx[:, :], None, ALU.is_ge),
                    lambda e: e.tensor_add(g_mask[:, :], g_mask[:, :], g_mrow[:, :]),
                    lambda e: e.scalar_tensor_tensor(g_bias[:, :], g_mrow[:, :],
                                                     -1e9, g_bias[:, :],
                                                     ALU.mult, ALU.add),
                ]
            ops.append(lambda e, tsub=tsub: e.tensor_mul(
                CW[:, tsub * 16:(tsub + 1) * 16], g_probs[:, :], g_mask[:, :]))
            tcw = small_chain(sV, ops)
            wt("tensor", sV, tcw)
            if tsub > 0 or ot > 0:
                wt("tensor", sA, cwt_ev)
            ttr = sT.inc(1)
            em("tensor", lambda e, tsub=tsub: e.transpose(
                PSB[7][0:16, 0:128], CW[:, tsub * 16:(tsub + 1) * 16],
                eyeF[:, :]).then_inc(sT.h, 1))
            wt("scalar", sT, ttr)
            em("scalar", lambda e, tsub=tsub: e.copy(
                cwT_f[:, tsub * 128:(tsub + 1) * 128], PSB[7][0:16, 0:128]))
            cwt_ev = sA.inc(1)
            em("scalar", lambda e, tsub=tsub: e.copy(
                cwT_b[:, tsub * 128:(tsub + 1) * 128],
                PSB[7][0:16, 0:128]).then_inc(sA.h, 1))
        gate_done = sA.n

        if _DBG == "p3o":
            barrier()
            if ot == NOT - 1:
                dma("gpsimd", sSt, dbg_out["dxln"], xlnscr)
                wt("gpsimd", sSt, sSt.n)
                return nc, prog
            continue

        if _DBG == "p3" and ot == NOT - 1:
            barrier()
            dma("gpsimd", sSt, dbg_out["dxln"], xlnscr)
            wt("gpsimd", sSt, sSt.n)
            return nc, prog

        wt("vector", sG2, ln_sp[3])
        wt("vector", sA, gate_done)
        wt("tensor", sW5x, fm_tick[3])
        if ot == 0:
            wt("scalar", sG, pack_done_g)
            wt("scalar", sW, pack_done_w)
            wt("vector", sG, pack_done_g)
            wt("vector", sW, pack_done_w)
        cwrow_cons = 0
        y_ev = {}
        tcb = 0
        for e_i in range(NE + 1):
            shared = e_i == NE
            nhc = NSC if shared else NHC
            csl = e_i % 2
            hev = {}
            for hc in range(nhc):
                wsl = hc % 2
                wt("sync", sT, w1s_cons[wsl])
                if ot == 0:
                    wt("sync", sW, pk_tick[("ws1", hc) if shared else ("w1", e_i, hc)])
                    wt("sync", sW, pk_tick[("ws3", hc) if shared else ("w3", e_i, hc)])
                t_w1 = dma("sync", sW5x, W1S[wsl][:, :],
                           pks1_v[hc] if shared else pk1_v[e_i, hc])
                wt("sync", sT, w3s_cons[wsl])
                t_w3 = dma("sync", sW5x, W3S[wsl][:, :],
                           pks3_v[hc] if shared else pk3_v[e_i, hc])
                b1k = hc % 2
                b3k = 2 + hc % 2
                wt("tensor", sW5x, t_w3)
                if hc >= 2:
                    wt("tensor", sA, hev[hc - 2][0])
                    wt("tensor", sV, hev[hc - 2][1])
                for dc in range(NDC):
                    mm(b1k, W1S[wsl][:, dc * 128:(dc + 1) * 128],
                       XLNFM[:, dc * 512:(dc + 1) * 512], dc == 0, dc == NDC - 1)
                for dc in range(NDC):
                    tmm = mm(b3k, W3S[wsl][:, dc * 128:(dc + 1) * 128],
                             XLNFM[:, dc * 512:(dc + 1) * 512], dc == 0,
                             dc == NDC - 1, tick=(dc == NDC - 1))
                w1s_cons[wsl] = w3s_cons[wsl] = tmm
                if hc == 0 and not shared:
                    wt("gpsimd", sA, gate_done)
                    if cwrow_cons:
                        wt("gpsimd", sT, cwrow_cons)
                    t_row = dma("gpsimd", sG2, cwrow[0:1, :], cwT_f[e_i:e_i + 1, :])
                    wt("tensor", sG2, t_row)
                    if cwb_cons[csl]:
                        wt("tensor", sV, cwb_cons[csl])
                    cwrow_cons = sT.inc(1)
                    em("tensor", lambda e: e.matmul(
                        PSB[4][:, :], ones_fr[0:1, :], cwrow[0:1, :],
                        start=True, stop=True).then_inc(sT.h, 1))
                    wt("scalar", sT, cwrow_cons)
                    tcb = sA.inc(1)
                    em("scalar", lambda e, d=CWBb[csl]:
                       e.copy(d[:, :], PSB[4][:, :]).then_inc(sA.h, 1))
                wt("scalar", sT, tmm)
                bcol = (bs1T[:, hc:hc + 1] if shared
                        else b1T[:, hc * 16 + e_i: hc * 16 + e_i + 1])
                ta = sA.inc(1)
                em("scalar", lambda e, b=b1k, d=SFT[hc % 2], bc=bcol:
                   e.activation(d[:, :], PSB[b][:, :], AF.Silu,
                                bias=bc).then_inc(sA.h, 1))
                wt("vector", sT, tmm)
                b3col = (bs3T[:, hc:hc + 1] if shared
                         else b3T[:, hc * 16 + e_i: hc * 16 + e_i + 1])
                if shared:
                    tv = sV.inc(1)
                    em("vector", lambda e, b=b3k, d=T3B[hc % 2], bc=b3col:
                       e.tensor_scalar_add(d[:, :], PSB[b][:, :],
                                           bc).then_inc(sV.h, 1))
                else:
                    wt("vector", sA, tcb)
                    tv = sV.inc(1)
                    em("vector", lambda e, b=b3k, d=T3B[hc % 2], bc=b3col,
                       cb=CWBb[csl]:
                       e.scalar_tensor_tensor(d[:, :], PSB[b][:, :], bc, cb[:, :],
                                              ALU.add, ALU.mult).then_inc(sV.h, 1))
                wt("vector", sA, ta)
                wt("vector", sV, tv)
                tv2 = sV.inc(1)
                em("vector", lambda e, d=SFT[hc % 2], t3=T3B[hc % 2], hc=hc:
                   e.tensor_mul(HSB[:, hc * 512:(hc + 1) * 512], d[:, :],
                                t3[:, :]).then_inc(sV.h, 1))
                hev[hc] = (ta, tv2)
            h_done = sV.n
            cwb_cons[csl] = sV.n
            if _DBG == "p5" and ot == 0 and e_i == 0:
                wt("gpsimd", sV, h_done)
                wt("gpsimd", sA, tcb)
                dma("gpsimd", sSt, dbg_out["dhs"], HSB[:, :])
                dma("gpsimd", sSt, dbg_out["dcwb"], CWBb[csl][:, :])
                dma("gpsimd", sSt, dbg_out["dcw"], CW[:, :])
            for ds in range(4):
                wsl = ds % 2
                if shared:
                    wt("sync", sT, wbs_cons[0])
                    wt("sync", sT, wbs_cons[1])
                    if ot == 0:
                        wt("sync", sW, pk_tick[("ws2", 15)])
                    dma("sync", sW5x, WBS[0][:, :], pks2_v[ds][:, 0:8192])
                    t_w = dma("sync", sW5x, WBS[1][:, 0:3072],
                              pks2_v[ds][:, 8192:11264])
                else:
                    wt("sync", sT, wbs_cons[wsl])
                    if ot == 0:
                        wt("sync", sW, pk_tick[("w2", e_i, 15)])
                    t_w = dma("sync", sW5x, WBS[wsl][:, 0:nhc * 512],
                              pk2_v[e_i, ds])
                wt("tensor", sW5x, t_w)
                wt("tensor", sV, h_done)
                for tsub in range(4):
                    k = ds * 4 + tsub
                    bank = 5 + k % 2
                    prev = y_ev.get((e_i, k - 2))
                    if prev is None and e_i > 0:
                        prev = y_ev.get((e_i - 1, 14 + k))
                    if prev:
                        wt("tensor", sV, prev)
                    for hc in range(nhc):
                        last = (hc == nhc - 1) and not shared
                        if shared:
                            wsrc = (WBS[0][:, hc * 512:(hc + 1) * 512] if hc < 16
                                    else WBS[1][:, (hc - 16) * 512:(hc - 15) * 512])
                        else:
                            wsrc = WBS[wsl][:, hc * 512:(hc + 1) * 512]
                        tmm = mm(bank,
                                 HSB[:, hc * 512 + tsub * 128:
                                     hc * 512 + (tsub + 1) * 128],
                                 wsrc,
                                 hc == 0, last, tick=last)
                    if shared:
                        mm(bank, cwT_b[:, tsub * 128:(tsub + 1) * 128],
                           b2b[0:16, ds * 512:(ds + 1) * 512], False, False)
                        tmm = mm(bank, ones_br[0:1, :],
                                 bs2b[0:1, ds * 512:(ds + 1) * 512],
                                 False, True, tick=True)
                    wt("vector", sT, tmm)
                    yv = YSB[:, tsub * 2048 + ds * 512: tsub * 2048 + (ds + 1) * 512]
                    y_ev[(e_i, k)] = sV.inc(1)
                    if e_i == 0:
                        em("vector", lambda e, b=bank, yv=yv:
                           e.tensor_copy(yv, PSB[b][:, :]).then_inc(sV.h, 1))
                    else:
                        em("vector", lambda e, b=bank, yv=yv:
                           e.tensor_add(yv, yv, PSB[b][:, :]).then_inc(sV.h, 1))
                if shared:
                    wbs_cons[0] = wbs_cons[1] = sT.n
                else:
                    wbs_cons[wsl] = sT.n
            if _DBG == "p5" and ot == 0 and e_i == 0:
                for kk_ in range(16):
                    wt("gpsimd", sV, y_ev[(0, kk_)])
                dma("gpsimd", sSt, dbg_out["dys"], YSB[:, :])
                wt("gpsimd", sSt, sSt.n)
                return nc, prog
        xlnfm_cons = sT.n

        for tsub in range(4):
            t_xl = dma("gpsimd", sG2, XTM[:, tsub * 2048:(tsub + 1) * 2048],
                       xlnscr[ot * 512 + tsub * 128: ot * 512 + (tsub + 1) * 128, :])
            for ds in range(4):
                wt("vector", sV, y_ev[(NE, ds * 4 + tsub)])
            wt("vector", sG2, t_xl)
            yv = YSB[:, tsub * 2048:(tsub + 1) * 2048]
            xv = XTM[:, tsub * 2048:(tsub + 1) * 2048]
            t_add = sV.inc(1)
            em("vector", lambda e, yv=yv, xv=xv:
               e.scalar_tensor_tensor(yv, xv, 1.0, yv, ALU.mult, ALU.add,
                                      accum_out=st_s1[:, :]).then_inc(sV.h, 1))
            t_ln = layer_norm(yv, xv, xv, t_add)
            wt("gpsimd", sV, t_ln)
            dma("gpsimd", sSt,
                out_h[ot * 512 + tsub * 128: ot * 512 + (tsub + 1) * 128, :], xv)
        barrier(CORE_SEMS)

    wt("gpsimd", sSt, sSt.n)
    return nc, prog


def _run_block(nc, prog):
    with nc.Block() as block:
        @block.sync
        def _(e):
            for fn in prog["sync"]:
                fn(e)

        @block.tensor
        def _(e):
            for fn in prog["tensor"]:
                fn(e)

        @block.scalar
        def _(e):
            for fn in prog["scalar"]:
                fn(e)

        @block.vector
        def _(e):
            for fn in prog["vector"]:
                fn(e)

        @block.gpsimd
        def _(e):
            for fn in prog["gpsimd"]:
                fn(e)
    return nc


def _make_consts():
    mask = np.zeros((128, 4 * 512), np.float32)
    kp = np.arange(128)[:, None]
    qf = np.arange(512)[None, :]
    for c in range(4):
        mask[:, c * 512:(c + 1) * 512] = (qf >= c * 128 + kp).astype(np.float32)
    eye = np.eye(128, dtype=np.float32)
    return mask, eye


def build():
    nc, prog = build_nc()
    _run_block(nc, prog)
    return nc


def make_in_maps(inputs):
    mask, eye = _make_consts()
    f = np.float32
    shared = {
        "wqkv": np.ascontiguousarray(np.asarray(inputs["L0_attn_w_qkv"], f)),
        "wout": np.ascontiguousarray(np.asarray(inputs["L0_attn_w_out"], f)),
        "gate_w": np.ascontiguousarray(np.asarray(inputs["L0_ffn_gate_w"], f)),
        "gate_b": np.ascontiguousarray(np.asarray(inputs["L0_ffn_gate_b"], f)).reshape(1, NE),
        "w1": np.ascontiguousarray(np.asarray(inputs["L0_ffn_w1"], f)),
        "b1": np.ascontiguousarray(np.asarray(inputs["L0_ffn_b1"], f)),
        "w2": np.ascontiguousarray(np.asarray(inputs["L0_ffn_w2"], f)),
        "b2": np.ascontiguousarray(np.asarray(inputs["L0_ffn_b2"], f)),
        "w3": np.ascontiguousarray(np.asarray(inputs["L0_ffn_w3"], f)),
        "b3": np.ascontiguousarray(np.asarray(inputs["L0_ffn_b3"], f)),
        "ws1": np.ascontiguousarray(np.asarray(inputs["L0_ffn_ws1"], f)),
        "bs1": np.ascontiguousarray(np.asarray(inputs["L0_ffn_bs1"], f)).reshape(1, HSH),
        "ws2": np.ascontiguousarray(np.asarray(inputs["L0_ffn_ws2"], f)),
        "bs2": np.ascontiguousarray(np.asarray(inputs["L0_ffn_bs2"], f)).reshape(1, DM),
        "ws3": np.ascontiguousarray(np.asarray(inputs["L0_ffn_ws3"], f)),
        "bs3": np.ascontiguousarray(np.asarray(inputs["L0_ffn_bs3"], f)).reshape(1, HSH),
        "mask": mask,
        "eye": eye,
    }
    x = np.asarray(inputs["L0_attn_x"], f)
    return [dict(shared, x=np.ascontiguousarray(x[i])) for i in range(8)]


def kernel(**inputs):
    nc = build()
    in_maps = make_in_maps(inputs)
    res = run_bass_kernel_spmd(nc, in_maps, core_ids=list(range(8)))
    return np.stack([np.asarray(r["out"]) for r in res.results], axis=0)
